# version 08
# baseline (speedup 1.0000x reference)
import numpy as np
import ml_dtypes
import concourse.bass as bass
import concourse.mybir as mybir
from concourse.bass_utils import run_bass_kernel_spmd

F32 = mybir.dt.float32
BF16 = mybir.dt.bfloat16
U8 = mybir.dt.uint8
AF = mybir.ActivationFunctionType
ALU = mybir.AluOpType
AX = mybir.AxisListType

NH, HS, NL, D, T, B = 16, 128, 512, 2048, 4096, 2
NTOK = 1024
NSLOT = 8
NE, NS, FF = 16, 2, 1024
NEXP = NE + NS
ISCALE = float(1.0 / np.sqrt(128.0))
NEG = -1.0e30
ARENA = 211968

MAP = {}
MAP['ident'] = 0
MAP['ones_bf'] = 256
MAP['ones_f'] = 2304
MAP['rows'] = 2816
MAP['amisc'] = 11008
MAP['x1'] = 12032
A0 = 44800
MAP['wbig'] = A0
MAP['wsm'] = A0 + 65536
MAP['wst0'] = A0 + 81920
MAP['keffo'] = A0 + 114688
MAP['vmato'] = A0 + 131072
MAP['wqt'] = A0
MAP['xst'] = A0 + 65536
MAP['lng'] = A0 + 98304
MAP['qbuf'] = A0 + 114688
MAP['tmp1a'] = A0 + 122880
MAP['rstd1a'] = A0 + 139264
MAP['keff_sb'] = A0 + 143360
MAP['qlbuf'] = A0 + 159744
MAP['wdkvt'] = A0
MAP['xst_b'] = A0 + 16384
MAP['lng_b'] = A0 + 49152
MAP['tmp1b'] = A0 + 65536
MAP['rstd1b'] = A0 + 81920
MAP['ckv_tm'] = 136192
MAP['ckv_fm'] = 168960
MAP['pbuf'] = A0
MAP['pt'] = A0 + 16384
MAP['qlath'] = A0 + 24576
MAP['vmath'] = A0 + 40960
MAP['masks'] = A0 + 43008
MAP['ylat_all'] = A0 + 51200
MAP['ylatt_all'] = A0 + 67584
MAP['pbuf3'] = A0 + 75776
MAP['rw2'] = 12032
MAP['ln2'] = A0
MAP['hsl'] = A0 + 32768
MAP['rgb_'] = A0 + 49152
MAP['rw1b'] = A0 + 57344
MAP['tmp3'] = A0 + 32768
MAP['gsl'] = A0 + 65536
MAP['cfm'] = A0 + 69632
MAP['cbf'] = A0 + 71680
MAP['cb'] = A0 + 72192
MAP['aff'] = A0 + 76288
MAP['wab'] = A0 + 76800
MAP['rgbT'] = A0 + 77312
MAP['rb1T'] = A0 + 77824
MAP['sgbT'] = A0 + 78336
MAP['sb1T'] = A0 + 78848
MAP['rb2b'] = A0 + 79360
MAP['sb2b'] = A0 + 83456
MAP['sqs'] = A0 + 87552
MAP['tks'] = A0 + 88576
MAP['rstd3'] = A0 + 89088
MAP['stg3'] = A0 + 93184
MAP['acc'] = A0 + 101376
assert MAP['acc'] + 65536 <= ARENA


def build_program():
    nc = bass.Bass("TRN2", target_bir_lowering=False)

    ap_ = lambda h: h.ap()
    xall = ap_(nc.dram_tensor('xall', [D, T], F32, kind='ExternalInput'))
    xown = ap_(nc.dram_tensor('xown', [D, NTOK], F32, kind='ExternalInput'))
    wq = ap_(nc.dram_tensor('wq', [D, D], F32, kind='ExternalInput'))
    wqt = ap_(nc.dram_tensor('wqt', [D, D], F32, kind='ExternalInput'))
    wuk = ap_(nc.dram_tensor('wuk', [D, NL], F32, kind='ExternalInput'))
    wuv = ap_(nc.dram_tensor('wuv', [D, NL], F32, kind='ExternalInput'))
    wot = ap_(nc.dram_tensor('wot', [D, D], F32, kind='ExternalInput'))
    wdkvt = ap_(nc.dram_tensor('wdkvt', [D, NL], F32, kind='ExternalInput'))
    rg = ap_(nc.dram_tensor('rg', [NE, D, FF], F32, kind='ExternalInput'))
    rw1 = ap_(nc.dram_tensor('rw1', [NE, D, FF], F32, kind='ExternalInput'))
    rw2 = ap_(nc.dram_tensor('rw2', [NE, FF, D], F32, kind='ExternalInput'))
    sg = ap_(nc.dram_tensor('sg', [NS, D, FF], F32, kind='ExternalInput'))
    sw1 = ap_(nc.dram_tensor('sw1', [NS, D, FF], F32, kind='ExternalInput'))
    sw2 = ap_(nc.dram_tensor('sw2', [NS, FF, D], F32, kind='ExternalInput'))
    wa = ap_(nc.dram_tensor('wa', [D, NE], F32, kind='ExternalInput'))
    rgbT = ap_(nc.dram_tensor('rgbT', [128, 8, NE], F32, kind='ExternalInput'))
    rb1T = ap_(nc.dram_tensor('rb1T', [128, 8, NE], F32, kind='ExternalInput'))
    sgbT = ap_(nc.dram_tensor('sgbT', [128, 8, NS], F32, kind='ExternalInput'))
    sb1T = ap_(nc.dram_tensor('sb1T', [128, 8, NS], F32, kind='ExternalInput'))
    rb2 = ap_(nc.dram_tensor('rb2', [NE, D], F32, kind='ExternalInput'))
    sb2 = ap_(nc.dram_tensor('sb2', [NS, D], F32, kind='ExternalInput'))
    identD = ap_(nc.dram_tensor('identD', [128, 128], BF16, kind='ExternalInput'))
    masksD = ap_(nc.dram_tensor('masksD', [128, 8, 512], BF16, kind='ExternalInput'))
    o = ap_(nc.dram_tensor('o', [D, NTOK], F32, kind='ExternalOutput'))
    qlscr = ap_(nc.dram_tensor('qlscr', [NH, 128, 4, NTOK], BF16, kind='Internal'))
    kescr = ap_(nc.dram_tensor('kescr', [128, 16, 512], BF16, kind='Internal'))
    vmscr = ap_(nc.dram_tensor('vmscr', [128, 4, 2048], BF16, kind='Internal'))

    arena = nc.alloc_sbuf_tensor('arena', [128, ARENA], U8)
    AR = arena.ap()

    def carve(name, shape, dtype):
        off = MAP[name]
        es = mybir.dt.size(dtype)
        n = int(np.prod(shape[1:])) * es
        v = AR[:, off:off + n].bitcast(dtype)
        if len(shape) == 3:
            v = v.rearrange('p (a b) -> p a b', a=shape[1])
        return v

    def carve_at(off, nbytes, dtype):
        return AR[:, off:off + nbytes].bitcast(dtype)

    pball = nc.alloc_psum_tensor('pball', [128, 8, 512], F32)
    PS = pball.ap()
    bankbf = lambda i: PS[:, i, :].bitcast(BF16)

    def bankf(i, lo=0, hi=512):
        return PS[:, i, lo:hi]

    class Sem:
        __slots__ = ('h', 'n')
        def __init__(self, name):
            self.h = nc.alloc_semaphore(name)
            self.n = 0
        def inc(self, instr, k=1):
            instr.then_inc(self.h, k)
            self.n += k
            return instr

    SW = Sem('sw'); SQO = Sem('sqo')
    SXp = [Sem('sx0'), Sem('sx1')]
    SQLp = [Sem('sql0'), Sem('sql1')]
    SWp = [Sem('sw0'), Sem('sw1')]
    SWs = [Sem('sws0'), Sem('sws1'), Sem('sws2')]
    SQIp = [Sem('sqi0'), Sem('sqi1')]
    SSM = Sem('ssm'); SOUT = Sem('sout')
    SPE = Sem('spe'); SAC = Sem('sac'); SDV = Sem('sdv'); SPO = Sem('spo')

    pe, ve, ac, po, sy = nc.tensor, nc.vector, nc.scalar, nc.gpsimd, nc.sync

    lastw = {}

    def wge(eng, s, val=None):
        v = s.n if val is None else val
        if v <= 0:
            return
        k = (id(eng), id(s))
        if lastw.get(k, -1) >= v:
            return
        eng.wait_ge(s.h, v)
        lastw[k] = v

    def dve_edge():
        wge(ve, SDV)

    def ac_edge():
        wge(ac, SAC)

    for s in (SW, SXp[0], SXp[1], SQLp[0], SQLp[1], SWp[0], SWp[1], SWs[0], SWs[1], SWs[2],
              SQIp[0], SQIp[1], SQO, SSM, SOUT, SPE, SAC, SDV, SPO):
        po.sem_clear(s.h)
    nc.all_engine_barrier()

    ident = carve('ident', [128, 128], BF16)
    ones_bf = carve('ones_bf', [128, 1024], BF16)
    ones_f = carve('ones_f', [128, 128], F32)
    rows = carve('rows', [128, 2048], F32)
    amisc = carve('amisc', [128, 256], F32)
    sqpp = [carve_at(MAP['amisc'], 512, BF16), carve_at(MAP['amisc'] + 512, 512, BF16)]
    x1 = carve('x1', [128, 16, 1024], BF16)

    SSM.inc(sy.dma_start(out=ident, in_=identD), 16)
    SSM.inc(po.memset(ones_bf, 1.0), 1)
    SSM.inc(po.memset(ones_f, 1.0), 1)

    lw_last = {}

    def load_weight_bf(src, dest_bf, nrow_chunks, cols, stage_off, stage_cap):
        per = max(1, stage_cap // cols)
        stg = [carve_at(stage_off + i * stage_cap * 4, stage_cap * 4, F32) for i in (0, 1)]
        done, i, pend = 0, 0, []
        src3 = src.rearrange('(c p) f -> p c f', p=128)
        while done < nrow_chunks or pend:
            while done < nrow_chunks and len(pend) < 2:
                m = min(per, nrow_chunks - done)
                par = i % 2
                dst = stg[par][:, 0:m * cols].rearrange('p (c f) -> p c f', c=m)
                last = lw_last.get((stage_off, par))
                if last is not None:
                    wge(sy, last[0], last[1])
                SWp[par].inc(sy.dma_start(out=dst, in_=src3[:, done:done + m, :]), 16)
                pend.append((par, done, m, SWp[par].n))
                done += m
                i += 1
            sidx, st_c, m, swn = pend.pop(0)
            sv = stg[sidx][:, 0:m * cols]
            dv = dest_bf[:, st_c:st_c + m, :].rearrange('p a b -> p (a b)')
            if sidx == 0:
                wge(ac, SWp[sidx], swn)
                SAC.inc(ac.activation(dv, sv, AF.Copy), 1)
                lw_last[(stage_off, sidx)] = (SAC, SAC.n)
            else:
                wge(ve, SWp[sidx], swn)
                SDV.inc(ve.tensor_copy(dv, sv), 1)
                lw_last[(stage_off, sidx)] = (SDV, SDV.n)

    lnsq_gate = [0, 0]
    ln_rows_free = [0]
    ln_apply_po = [0, 0]
    ln_apply_ve = [0, 0]
    ln_tmp_free = [0]
    ln_bcast = [0, 0]

    def ln_S(x_f32, ntok, rstd_sb, mrs_sb, par):
        wge(pe, SXp[par])
        wge(ac, SXp[par])
        wge(pe, SAC)
        for c in range(16):
            ins = pe.matmul(PS[0:1, 0, 0:ntok], ones_f[:, 0:1], x_f32[:, c, :],
                            start=(c == 0), stop=(c == 15))
        SPE.inc(ins, 1)
        for c in range(16):
            sq = sqpp[c % 2]
            wge(ac, SPE, lnsq_gate[c % 2])
            SAC.inc(ac.activation(sq[:, 0:ntok], x_f32[:, c, :], AF.Square), 1)
            wge(pe, SAC)
            ins = pe.matmul(PS[0:1, 1, 0:ntok], ones_bf[:, 0:1], sq[:, 0:ntok],
                            start=(c == 0), stop=(c == 15))
            SPE.inc(ins, 1)
            lnsq_gate[c % 2] = SPE.n
        mean = rows[0:1, 0:ntok]
        msq = rows[0:1, 512:512 + ntok]
        mean2 = rows[0:1, 1024:1024 + ntok]
        rstd = rows[0:1, 1536:1536 + ntok]
        wge(ac, SPE)
        wge(ac, SDV, ln_rows_free[0])
        SAC.inc(ac.activation(mean, PS[0:1, 0, 0:ntok], AF.Copy, scale=1.0 / D), 1)
        SAC.inc(ac.activation(msq, PS[0:1, 1, 0:ntok], AF.Copy, scale=1.0 / D), 1)
        ac_edge()
        SAC.inc(ac.activation(mean2, mean, AF.Square), 1)
        wge(ve, SAC)
        SDV.inc(ve.scalar_tensor_tensor(msq, msq, 1.0e-5, mean2,
                                        ALU.add, ALU.subtract), 1)
        wge(ac, SDV)
        SAC.inc(ac.activation(mean2, msq, AF.Ln), 1)
        ac_edge()
        SAC.inc(ac.activation(rstd, mean2, AF.Exp, scale=-0.5), 1)
        ac_edge()
        wge(ve, SAC)
        SDV.inc(ve.tensor_mul(mean2, mean, rstd), 1)
        ln_rows_free[0] = SDV.n
        wge(pe, SAC)
        wge(pe, SDV)
        SPE.inc(pe.matmul(PS[:, 6, 0:ntok], ones_f[0:1, :], rstd, start=True, stop=True), 1)
        SPE.inc(pe.matmul(PS[:, 7, 0:ntok], ones_f[0:1, :], mean2, start=True, stop=True), 1)
        wge(ac, SPE)
        wge(ac, SPO, ln_apply_po[par])
        wge(ac, SDV, ln_apply_ve[par])
        SAC.inc(ac.activation(rstd_sb[:, 0:ntok], PS[:, 6, 0:ntok], AF.Copy), 1)
        SAC.inc(ac.activation(mrs_sb[:, 0:ntok], PS[:, 7, 0:ntok], AF.Copy), 1)
        ln_bcast[par] = SAC.n

    def ln_T(x_f32, ln_out, ntok, rstd_sb, mrs_sb, tmp_big, par, lng_free_val):
        wge(po, SAC, ln_bcast[par])
        wge(po, SDV, ln_tmp_free[0])
        for c in range(16):
            ins = po.tensor_mul(tmp_big[:, c, 0:ntok], x_f32[:, c, :], rstd_sb[:, 0:ntok])
        SPO.inc(ins, 1)
        ln_apply_po[par] = SPO.n
        wge(ve, SPO)
        wge(ve, SPE, lng_free_val)
        for c in range(16):
            ins = ve.tensor_sub(ln_out[:, c, :], tmp_big[:, c, 0:ntok], mrs_sb[:, 0:ntok])
        SDV.inc(ins, 1)
        ln_apply_ve[par] = SDV.n
        ln_tmp_free[0] = SDV.n

    wbig = carve('wbig', [128, 16, 2048], BF16)
    wsm = carve('wsm', [128, 16, 512], BF16)
    keffo = carve('keffo', [128, 16, 512], BF16)
    vmato = carve('vmato', [128, 4, 2048], BF16)

    load_weight_bf(wq, wbig, 16, 2048, MAP['wst0'], 4096)
    load_weight_bf(wuk, wsm, 16, 512, MAP['wst0'], 4096)
    wge(pe, SAC)
    wge(pe, SDV)
    for pc in range(16):
        b = pc % 2
        wge(pe, SAC)
        for jc in range(16):
            ins = pe.matmul(bankf(b), wbig[:, jc, pc * 128:(pc + 1) * 128], wsm[:, jc, :],
                            start=(jc == 0), stop=(jc == 15))
        SPE.inc(ins, 1)
        wge(ac, SPE)
        SAC.inc(ac.activation(keffo[:, pc, :], bankf(b), AF.Copy), 1)
    wge(sy, SAC)
    SQO.inc(sy.dma_start(out=kescr, in_=keffo), 16)

    load_weight_bf(wot, wbig, 16, 2048, MAP['wst0'], 4096)
    load_weight_bf(wuv, wsm, 16, 512, MAP['wst0'], 4096)
    wge(pe, SAC)
    wge(pe, SDV)
    for lc in range(4):
        for ds in range(4):
            b = 2 + (lc * 4 + ds) % 2
            wge(pe, SAC)
            for mc in range(16):
                ins = pe.matmul(bankf(b), wsm[:, mc, lc * 128:(lc + 1) * 128],
                                wbig[:, mc, ds * 512:(ds + 1) * 512],
                                start=(mc == 0), stop=(mc == 15))
            SPE.inc(ins, 1)
            wge(ac, SPE)
            SAC.inc(ac.activation(vmato[:, lc, ds * 512:(ds + 1) * 512], bankf(b), AF.Copy), 1)
    wge(sy, SAC)
    SQO.inc(sy.dma_start(out=vmscr, in_=vmato), 16)

    nc.all_engine_barrier()

    wqt_sb = carve('wqt', [128, 16, 2048], BF16)
    load_weight_bf(wqt, wqt_sb, 16, 2048, MAP['wst0'], 4096)
    nc.all_engine_barrier()
    keff_sb = carve('keff_sb', [128, 16, 512], BF16)
    qlbuf = [carve_at(MAP['qlbuf'] + i * 2048, 2048, BF16) for i in (0, 1)]
    wge(sy, SQO)
    SSM.inc(sy.dma_start(out=keff_sb, in_=kescr), 16)
    wge(po, SQO)
    wge(ac, SQO)

    xst = [carve_at(MAP['xst'] + i * 16384, 16384, F32).rearrange('p (a b) -> p a b', a=16)
           for i in (0, 1)]
    lng = [carve_at(MAP['lng'] + i * 8192, 8192, BF16).rearrange('p (a b) -> p a b', a=16)
           for i in (0, 1)]
    qbuf = carve('qbuf', [128, 16, 256], BF16)
    tmp1a = carve('tmp1a', [128, 16, 256], F32)
    rstd1a = [carve_at(MAP['rstd1a'] + i * 2048, 1024, F32) for i in (0, 1)]
    mrs1a = [carve_at(MAP['rstd1a'] + i * 2048 + 1024, 1024, F32) for i in (0, 1)]

    cons_free = {2: 0, 3: 0, 4: 0, 5: 0}
    lng_free = [0, 0]
    sx_done = {}
    qb_free = [0]
    ql_free = [0]

    def stage1a_T(g):
        par = g % 2
        wge(po, SXp[g % 2], sx_done[g])
        SPO.inc(po.tensor_copy(x1[:, :, g * 256:(g + 1) * 256], xst[par]), 1)
        ln_T(xst[par], lng[par], 256, rstd1a[par], mrs1a[par], tmp1a, par,
             lng_free[par])
        wge(pe, SDV, ln_apply_ve[par])
        for jc in range(16):
            b = 2 + jc % 2
            wge(pe, SAC, cons_free[b])
            for ic in range(16):
                ins = pe.matmul(bankf(b, 0, 256), wqt_sb[:, ic, jc * 128:(jc + 1) * 128],
                                lng[par][:, ic, :], start=(ic == 0), stop=(ic == 15))
            SPE.inc(ins, 1)
            wge(ac, SPE)
            wge(ac, SPE, qb_free[0])
            SAC.inc(ac.activation(qbuf[:, jc, :], bankf(b, 0, 256), AF.Copy), 1)
            cons_free[b] = SAC.n
        lng_free[par] = SPE.n
        qv = SAC.n
        wge(pe, SSM)
        wge(pe, SAC, qv)
        for hh in range(16):
            wge(pe, SDV, ql_free[0])
            for lc in range(4):
                ins = pe.matmul(PS[:, 4 + lc // 2, (lc % 2) * 256:(lc % 2) * 256 + 256],
                                keff_sb[:, hh, lc * 128:(lc + 1) * 128],
                                qbuf[:, hh, :], start=True, stop=True)
            SPE.inc(ins, 1)
            wge(ve, SPE)
            wge(ve, SQLp[hh % 2])
            SDV.inc(ve.tensor_scalar(qlbuf[hh % 2][:, 0:512],
                                     PS[:, 4, :], ISCALE, None, ALU.mult), 1)
            SDV.inc(ve.tensor_scalar(qlbuf[hh % 2][:, 512:1024],
                                     PS[:, 5, :], ISCALE, None, ALU.mult), 1)
            ql_free[0] = SDV.n
            wge(sy, SDV, ql_free[0])
            SQLp[hh % 2].inc(
                sy.dma_start(out=qlscr[hh][:, :, g * 256:(g + 1) * 256],
                             in_=qlbuf[hh % 2].rearrange('p (a b) -> p a b', a=4)), 16)
        qb_free[0] = SPE.n

    for g in range(4):
        wge(sy, SPO)
        SXp[g % 2].inc(sy.dma_start(out=xst[g % 2],
                            in_=xown[:, g * 256:(g + 1) * 256].rearrange('(c p) t -> p c t', p=128)), 16)
        sx_done[g] = SXp[g % 2].n
        ln_S(xst[g % 2], 256, rstd1a[g % 2], mrs1a[g % 2], g % 2)
        if g >= 1:
            stage1a_T(g - 1)
    stage1a_T(3)

    nc.all_engine_barrier()

    wdkvt_sb = carve('wdkvt', [128, 16, 512], BF16)
    load_weight_bf(wdkvt, wdkvt_sb, 16, 512, MAP['xst_b'], 4096)
    nc.all_engine_barrier()
    wge(ac, SQO)

    xstb = [carve_at(MAP['xst_b'] + i * 16384, 16384, F32).rearrange('p (a b) -> p a b', a=16)
            for i in (0, 1)]
    lngb = [carve_at(MAP['lng_b'] + i * 8192, 8192, BF16).rearrange('p (a b) -> p a b', a=16)
            for i in (0, 1)]
    tmp1b = carve('tmp1b', [128, 16, 256], F32)
    rstd1b = [carve_at(MAP['rstd1b'] + i * 2048, 1024, F32) for i in (0, 1)]
    mrs1b = [carve_at(MAP['rstd1b'] + i * 2048 + 1024, 1024, F32) for i in (0, 1)]
    ckv_tm = carve('ckv_tm', [128, 32, 512], BF16)
    ckv_fm = carve('ckv_fm', [128, 4, 4096], BF16)

    def stage1b_T(g):
        par = g % 2
        ln_T(xstb[par], lngb[par], 256, rstd1b[par], mrs1b[par], tmp1b, par,
             lng_free[par])
        wge(pe, SDV, ln_apply_ve[par])
        for kt in range(2):
            b = 2 + kt
            wge(pe, SAC, cons_free[b])
            for ic in range(16):
                ins = pe.matmul(bankf(b), lngb[par][:, ic, kt * 128:(kt + 1) * 128],
                                wdkvt_sb[:, ic, :], start=(ic == 0), stop=(ic == 15))
            SPE.inc(ins, 1)
            wge(ac, SPE)
            SAC.inc(ac.activation(ckv_tm[:, g * 2 + kt, :], bankf(b), AF.Copy), 1)
            cons_free[b] = SAC.n
        for lc in range(4):
            b = 4 + lc % 2
            wge(pe, SAC, cons_free[b])
            for ic in range(16):
                ins = pe.matmul(bankf(b, 0, 256), wdkvt_sb[:, ic, lc * 128:(lc + 1) * 128],
                                lngb[par][:, ic, :], start=(ic == 0), stop=(ic == 15))
            SPE.inc(ins, 1)
            wge(ac, SPE)
            SAC.inc(ac.activation(ckv_fm[:, lc, g * 256:(g + 1) * 256], bankf(b, 0, 256), AF.Copy), 1)
            cons_free[b] = SAC.n
        lng_free[par] = SPE.n

    for g in range(16):
        wge(sy, SPO)
        SXp[g % 2].inc(sy.dma_start(out=xstb[g % 2],
                            in_=xall[:, g * 256:(g + 1) * 256].rearrange('(c p) t -> p c t', p=128)), 16)
        ln_S(xstb[g % 2], 256, rstd1b[g % 2], mrs1b[g % 2], g % 2)
        if g >= 1:
            stage1b_T(g - 1)
    stage1b_T(15)

    nc.all_engine_barrier()

    pbuf = [carve_at(MAP['pbuf'] + i * 8192, 8192, BF16) for i in (0, 1)] + \
           [carve_at(MAP['pbuf3'], 8192, BF16)]
    ptb = carve('pt', [128, 32, 128], BF16)
    qlath = [carve_at(MAP['qlath'] + i * 8192, 8192, BF16).rearrange('p (a b) -> p a b', a=4)
             for i in (0, 1)]
    vmath = [carve_at(MAP['vmath'] + i * 1024, 1024, BF16).rearrange('p (a b) -> p a b', a=4)
             for i in (0, 1)]
    ylat_all = [carve_at(MAP['ylat_all'] + i * 8192, 8192, BF16).rearrange('p (a b) -> p a b', a=8)
                for i in (0, 1)]
    ylatt_all = carve('ylatt_all', [128, 4, 1024], BF16)
    masks = carve('masks', [128, 8, 512], BF16)
    nmx = [amisc[:, 80 * p:80 * p + 16] for p in range(3)]
    sums = [amisc[:, 80 * p + 16:80 * p + 32] for p in range(3)]
    rr = [amisc[:, 80 * p + 32:80 * p + 48] for p in range(3)]
    spr = [amisc[:, 80 * p + 48:80 * p + 64] for p in range(3)]
    negm = [amisc[:, 80 * p + 64:80 * p + 65] for p in range(3)]
    stot = [amisc[:, 80 * p + 65:80 * p + 66] for p in range(3)]
    recip = [amisc[:, 80 * p + 66:80 * p + 67] for p in range(3)]

    wge(sy, SQO)
    wge(sy, SQLp[0])
    wge(sy, SQLp[1])
    SSM.inc(sy.dma_start(out=masks, in_=masksD), 16)

    qh_free = [0, 0]

    def load_head(h):
        par = h % 2
        wge(sy, SPE, qh_free[par])
        SQIp[par].inc(sy.dma_start(out=qlath[par], in_=qlscr[h]), 16)
        SQIp[par].inc(sy.dma_start(out=vmath[par], in_=vmscr[:, :, h * 128:(h + 1) * 128]), 16)

    load_head(0)
    wge(pe, SSM)
    snap_p_free = [0, 0, 0]
    snap_sb = [0, 0, 0]
    nm_free = [0, 0, 0]
    tb_free = {3: (SDV, 0), 7: (SDV, 0)}
    b45_free = {4: 0, 5: 0}


    def attn_a(it, h, si):
        c = si + 1
        hp = h % 2
        par = it % 3
        if si == 0:
            wge(pe, SQIp[hp], 32 * (h // 2 + 1))
        if si == 3 and h + 1 < NH:
            load_head(h + 1)
        wge(ve, SAC, nm_free[par])
        for kc in range(c):
            b = kc % 3
            wge(pe, SAC, snap_sb[b])
            for lc in range(4):
                ins = pe.matmul(bankf(b), qlath[hp][:, lc, si * 128:(si + 1) * 128],
                                ckv_fm[:, lc, kc * 512:(kc + 1) * 512],
                                start=(lc == 0), stop=(lc == 3))
            SPE.inc(ins, 1)
            wge(ve, SPE)
            if kc == c - 1:
                SDV.inc(ve.tensor_tensor(bankf(b), bankf(b), masks[:, si, :], ALU.add), 1)
                dve_edge()
            SDV.inc(ve.reduce_max(nmx[par][:, kc:kc + 1], bankf(b), axis=AX.X, negate=True), 1)
            wge(ac, SDV)
            wge(ac, SPE, snap_p_free[par])
            SAC.inc(ac.activation(pbuf[par][:, kc * 512:(kc + 1) * 512], bankf(b), AF.Exp,
                                  bias=nmx[par][:, kc:kc + 1],
                                  accum_out=sums[par][:, kc:kc + 1]), 1)
            snap_sb[b] = SAC.n
        wge(ve, SAC)
        SDV.inc(ve.tensor_reduce(negm[par], nmx[par][:, 0:c], AX.X, ALU.min), 1)
        dve_edge()
        wge(ac, SDV)
        SAC.inc(ac.activation(rr[par][:, 0:c], nmx[par][:, 0:c], AF.Exp,
                              bias=negm[par], scale=-1.0), 1)
        nm_free[par] = SAC.n
        wge(ve, SAC)
        SDV.inc(ve.tensor_mul(spr[par][:, 0:c], sums[par][:, 0:c], rr[par][:, 0:c]), 1)
        dve_edge()
        SDV.inc(ve.reduce_sum(stot[par], spr[par][:, 0:c], axis=AX.X), 1)
        dve_edge()
        SDV.inc(ve.reciprocal(recip[par], stot[par]), 1)
        for kc in range(c):
            SDV.inc(ve.tensor_scalar(pbuf[par][:, kc * 512:(kc + 1) * 512],
                                     pbuf[par][:, kc * 512:(kc + 1) * 512],
                                     rr[par][:, kc:kc + 1], None, ALU.mult), 1)
        return dict(c=c, hp=hp, par=par, h=h, si=si, it=it, p_ready=SDV.n)

    def attn_b(st):
        c, hp, par, h, si = st['c'], st['hp'], st['par'], st['h'], st['si']
        wge(pe, SDV, st['p_ready'])
        for j4 in range(c):
            bb = 3 if j4 % 2 == 0 else 7
            wge(pe, tb_free[bb][0], tb_free[bb][1])
            for jj in range(4):
                ins = pe.transpose(bankbf(bb)[:, jj * 128:(jj + 1) * 128],
                                   pbuf[par][:, (j4 * 4 + jj) * 128:(j4 * 4 + jj + 1) * 128],
                                   ident)
            SPE.inc(ins, 1)
            wge(ve, SPE)
            SDV.inc(ve.tensor_copy(
                ptb[:, j4 * 4:(j4 + 1) * 4, :].rearrange('p a b -> p (a b)'),
                bankbf(bb)[:, 0:512]), 1)
            tb_free[bb] = (SDV, SDV.n)
        snap_p_free[par] = SPE.n
        pvb = 4 + st['it'] % 2
        wge(pe, tb_free[3][0], tb_free[3][1])
        wge(pe, tb_free[7][0], tb_free[7][1])
        wge(pe, SDV, b45_free[pvb])
        for j in range(4 * c):
            ins = pe.matmul(bankf(pvb), ptb[:, j, :], ckv_tm[:, j, :],
                            start=(j == 0), stop=(j == 4 * c - 1))
        SPE.inc(ins, 1)
        wge(ve, SPE)
        SDV.inc(ve.tensor_scalar(ylat_all[hp][:, si, :], bankf(pvb),
                                 recip[par][:, 0:1], None, ALU.mult), 1)
        b45_free[pvb] = SDV.n

    def attn_epi(h):
        hp = h % 2
        ya = max(b45_free[4], b45_free[5])
        for lc in range(4):
            bb = 3 if lc % 2 == 0 else 7
            wge(pe, SDV, ya)
            wge(pe, tb_free[bb][0], tb_free[bb][1])
            for si in range(8):
                ins = pe.transpose(bankbf(bb)[:, si * 128:(si + 1) * 128],
                                   ylat_all[h % 2][:, si, lc * 128:(lc + 1) * 128], ident)
            SPE.inc(ins, 1)
            wge(ac, SPE)
            SAC.inc(ac.activation(ylatt_all[:, lc, :], bankbf(bb)[:, 0:1024],
                                  AF.Copy), 1)
            tb_free[bb] = (SAC, SAC.n)
        ylt = SAC.n
        for th in range(2):
            bb = 4 + th
            wge(pe, SAC, ylt)
            wge(pe, SDV, b45_free[bb])
            for lc in range(4):
                ins = pe.matmul(PS[:, bb, :], vmath[hp][:, lc, :],
                                ylatt_all[:, lc, th * 512:(th + 1) * 512],
                                start=(lc == 0), stop=(lc == 3))
            SPE.inc(ins, 1)
            wge(ve, SPE)
            SDV.inc(ve.tensor_tensor(x1[:, h, th * 512:(th + 1) * 512],
                                     x1[:, h, th * 512:(th + 1) * 512],
                                     PS[:, bb, :], ALU.add), 1)
            b45_free[bb] = SDV.n
        qh_free[hp] = SPE.n

    slots = [(h, si) for h in range(NH) for si in range(NSLOT)]
    pend2 = []
    epi_pend = []
    for it, (h, si) in enumerate(slots):
        if len(pend2) >= 3:
            stb = pend2.pop(0)
            attn_b(stb)
            if epi_pend:
                attn_epi(epi_pend.pop(0))
            if stb['si'] == NSLOT - 1:
                epi_pend.append(stb['h'])
        pend2.append(attn_a(it, h, si))
    for stb in pend2:
        attn_b(stb)
        if epi_pend:
            attn_epi(epi_pend.pop(0))
        if stb['si'] == NSLOT - 1:
            epi_pend.append(stb['h'])
    for hh2 in epi_pend:
        attn_epi(hh2)

    nc.all_engine_barrier()

    ln2 = carve('ln2', [128, 16, 1024], BF16)
    hsl = carve('hsl', [128, 8, 1024], BF16)
    rgb_ = [carve_at(MAP['rgb_'] + i * 4096, 4096, BF16).rearrange('p (a b) -> p a b', a=16)
            for i in (0, 1)]
    rw1b = [carve_at(MAP['rw1b'] + i * 4096, 4096, BF16).rearrange('p (a b) -> p a b', a=16)
            for i in (0, 1)]
    rw2b = carve('rw2', [128, 8, 2048], BF16)
    gsl = [carve_at(MAP['gsl'] + i * 2048, 2048, BF16) for i in (0, 1)]
    cfm = carve('cfm', [128, 1024], BF16)
    cbf = carve('cbf', [128, 8, 16], BF16)
    cbuf = [carve_at(MAP['cb'] + i * 2048, 2048, BF16) for i in (0, 1)]
    aff = carve('aff', [128, 8, 16], F32)
    wab = carve('wab', [128, 16, 16], BF16)
    rgbT_sb = carve('rgbT', [128, 8, 16], F32)
    rb1T_sb = carve('rb1T', [128, 8, 16], F32)
    sgbT_sb = carve('sgbT', [128, 8, 2], F32)
    sb1T_sb = carve('sb1T', [128, 8, 2], F32)
    rb2b = carve('rb2b', [128, 2048], BF16)
    sb2b = carve('sb2b', [128, 2048], BF16)
    sqs = carve('sqs', [128, 512], BF16)
    tks = carve('tks', [128, 64], F32)
    rstd3 = carve_at(MAP['rstd3'], 2048, F32)
    mrs3 = carve_at(MAP['rstd3'] + 2048, 2048, F32)
    acc = carve('acc', [128, 16, 1024], F32)
    tmp3 = carve('tmp3', [128, 16, 512], F32)

    for dst, srcD in ((rgbT_sb, rgbT), (rb1T_sb, rb1T), (sgbT_sb, sgbT), (sb1T_sb, sb1T)):
        SSM.inc(sy.dma_start(out=dst, in_=srcD), 16)
    wa_stage = rows[:, 0:256].rearrange('p (a b) -> p a b', a=16)
    SSM.inc(sy.dma_start(out=wa_stage, in_=wa.rearrange('(c p) e -> p c e', p=128)), 16)
    wge(po, SSM)
    SPO.inc(po.tensor_copy(wab, wa_stage), 1)
    wge(sy, SPO)
    SSM.inc(sy.dma_start(out=rows[0:16, 0:2048], in_=rb2), 16)
    wge(po, SSM)
    SPO.inc(po.tensor_copy(rb2b[0:16, :], rows[0:16, 0:2048]), 1)
    wge(sy, SPO)
    SSM.inc(sy.dma_start(out=rows[0:2, 0:2048], in_=sb2), 16)
    wge(po, SSM)
    SPO.inc(po.tensor_copy(sb2b[0:2, :], rows[0:2, 0:2048]), 1)

    wge(ac, SPO)
    for g in range(2):
        gs = slice(g * 512, (g + 1) * 512)
        wge(pe, SAC)
        for cch in range(16):
            wge(ac, SPE)
            SAC.inc(ac.activation(sqs, x1[:, cch, gs], AF.Square), 1)
            wge(pe, SAC)
            ins0 = pe.matmul(PS[0:1, 0, :], ones_bf[:, 0:1], x1[:, cch, gs],
                             start=(cch == 0), stop=(cch == 15))
            ins1 = pe.matmul(PS[0:1, 1, :], ones_bf[:, 0:1], sqs,
                             start=(cch == 0), stop=(cch == 15))
            SPE.inc(ins1, 1)
        mean = rows[0:1, 0:512]
        msq = rows[0:1, 512:1024]
        mean2 = rows[0:1, 1024:1536]
        rstd = rows[0:1, 1536:2048]
        wge(ac, SPE)
        SAC.inc(ac.activation(mean, PS[0:1, 0, :], AF.Copy, scale=1.0 / D), 1)
        SAC.inc(ac.activation(msq, PS[0:1, 1, :], AF.Copy, scale=1.0 / D), 1)
        ac_edge()
        SAC.inc(ac.activation(mean2, mean, AF.Square), 1)
        wge(ve, SAC)
        SDV.inc(ve.scalar_tensor_tensor(msq, msq, 1.0e-5, mean2,
                                        ALU.add, ALU.subtract), 1)
        wge(ac, SDV)
        SAC.inc(ac.activation(mean2, msq, AF.Ln), 1)
        ac_edge()
        SAC.inc(ac.activation(rstd, mean2, AF.Exp, scale=-0.5), 1)
        ac_edge()
        wge(ve, SAC)
        SDV.inc(ve.tensor_mul(mean2, mean, rstd), 1)
        wge(pe, SAC)
        wge(pe, SDV)
        SPE.inc(pe.matmul(PS[:, 2, :], ones_f[0:1, :], rstd, start=True, stop=True), 1)
        SPE.inc(pe.matmul(PS[:, 3, :], ones_f[0:1, :], mean2, start=True, stop=True), 1)
        wge(ac, SPE)
        SAC.inc(ac.activation(rstd3[:, 0:512], PS[:, 2, :], AF.Copy), 1)
        SAC.inc(ac.activation(mrs3[:, 0:512], PS[:, 3, :], AF.Copy), 1)
        wge(po, SAC)
        for cch in range(16):
            ins = po.tensor_mul(tmp3[:, cch, :], x1[:, cch, gs], rstd3[:, 0:512])
        SPO.inc(ins, 1)
        wge(ve, SPO)
        for cch in range(16):
            ins = ve.tensor_sub(ln2[:, cch, gs], tmp3[:, cch, :], mrs3[:, 0:512])
        SDV.inc(ins, 1)
        wge(po, SDV)
        SPO.inc(po.tensor_tensor(acc[:, :, gs], x1[:, :, gs], ln2[:, :, gs], ALU.add), 1)

    wge(pe, SDV)
    wge(pe, SPO)
    for tt in range(8):
        b = 4 + tt % 2
        wge(pe, SAC)
        wge(pe, SDV)
        for ic in range(16):
            ins = pe.matmul(PS[0:128, b, 0:16], ln2[:, ic, tt * 128:(tt + 1) * 128],
                            wab[:, ic, :], start=(ic == 0), stop=(ic == 15))
        SPE.inc(ins, 1)
        wge(ac, SPE)
        SAC.inc(ac.activation(aff[:, tt, :], PS[0:128, b, 0:16], AF.Sigmoid), 1)
        a_t = aff[:, tt, :]
        t16a, t16b = tks[:, 16:32], tks[:, 32:48]
        m1, m2, den, rp, p1, p2 = (tks[:, 0:1], tks[:, 1:2], tks[:, 2:3],
                                   tks[:, 3:4], tks[:, 4:5], tks[:, 5:6])
        wge(ve, SAC)
        SDV.inc(ve.reduce_max(m1, a_t, axis=AX.X), 1)
        dve_edge()
        SDV.inc(ve.tensor_scalar(t16a, a_t, m1, None, ALU.is_ge), 1)
        dve_edge()
        SDV.inc(ve.scalar_tensor_tensor(t16b, t16a, NEG, a_t, ALU.mult, ALU.add), 1)
        dve_edge()
        SDV.inc(ve.reduce_max(m2, t16b, axis=AX.X), 1)
        dve_edge()
        SDV.inc(ve.tensor_scalar(t16b, a_t, m2, None, ALU.is_ge), 1)
        dve_edge()
        SDV.inc(ve.tensor_sub(t16b, t16b, t16a), 1)
        SDV.inc(ve.tensor_add(den, m1, m2), 1)
        dve_edge()
        SDV.inc(ve.reciprocal(rp, den), 1)
        dve_edge()
        SDV.inc(ve.tensor_mul(p1, m1, rp), 1)
        SDV.inc(ve.tensor_mul(p2, m2, rp), 1)
        dve_edge()
        SDV.inc(ve.tensor_scalar(t16a, t16a, p1, None, ALU.mult), 1)
        SDV.inc(ve.tensor_scalar(t16b, t16b, p2, None, ALU.mult), 1)
        dve_edge()
        SDV.inc(ve.tensor_add(a_t, t16a, t16b), 1)
        dve_edge()
        SDV.inc(ve.tensor_copy(cbf[:, tt, :], a_t), 1)
    wge(pe, SDV)
    wge(pe, SAC)
    for tt in range(8):
        ins = pe.transpose(bankbf(6)[0:16, tt * 128:(tt + 1) * 128], cbf[:, tt, :], ident)
    SPE.inc(ins, 1)
    wge(ac, SPE)
    SAC.inc(ac.activation(cfm[0:16, :], bankbf(6)[0:16, :], AF.Copy), 1)

    wge(pe, SAC)
    for tt in range(8):
        bb = 2 + tt // 4
        ins = pe.matmul(PS[:, bb, (tt % 4) * 128:(tt % 4 + 1) * 128],
                        cbf[:, tt, 0:1].broadcast_to([128, 128]), ident,
                        start=True, stop=True)
    SPE.inc(ins, 1)
    wge(ac, SPE)
    SAC.inc(ac.activation(cbuf[0][:, 0:512], bankf(2), AF.Copy), 1)
    SAC.inc(ac.activation(cbuf[0][:, 512:1024], bankf(3), AF.Copy), 1)
    cb_gate = SAC.n

    def w_src(e, kind):
        if e < NE:
            return {'g': rg, 'w1': rw1, 'w2': rw2}[kind][e]
        return {'g': sg, 'w1': sw1, 'w2': sw2}[kind][e - NE]

    def gbias(e, fc):
        return rgbT_sb[:, fc, e:e + 1] if e < NE else sgbT_sb[:, fc, e - NE:e - NE + 1]

    def bbias(e, fc):
        return rb1T_sb[:, fc, e:e + 1] if e < NE else sb1T_sb[:, fc, e - NE:e - NE + 1]

    stg3 = [carve_at(MAP['rstd3'] + i * 4096, 4096, F32) for i in range(3)]
    stg_state = [None, None, None]
    stg_rr = [0]
    cast_rr = [0]

    def stream_chunk(src, dest_flat, reshape3, gate):
        s = stg_rr[0]
        stg_rr[0] = (s + 1) % 3
        if stg_state[s] is not None:
            wge(sy, stg_state[s][0], stg_state[s][1])
        dview = stg3[s].rearrange('p (c f) -> p c f', c=8) if reshape3 else stg3[s]
        SWs[s].inc(sy.dma_start(out=dview, in_=src), 16)
        if cast_rr[0] == 0:
            wge(ac, SWs[s])
            wge(ac, SPE, gate)
            SAC.inc(ac.activation(dest_flat, stg3[s], AF.Copy), 1)
            stg_state[s] = (SAC, SAC.n)
        else:
            wge(ve, SWs[s])
            wge(ve, SPE, gate)
            SDV.inc(ve.tensor_copy(dest_flat, stg3[s]), 1)
            stg_state[s] = (SDV, SDV.n)
        cast_rr[0] ^= 1

    def emit_wdma(e, fc, kind, dest, gate=0):
        if kind in ('g', 'w1'):
            src_full = w_src(e, kind)[:, fc * 128:(fc + 1) * 128]
            for hh in range(2):
                src = src_full[1024 * hh:1024 * (hh + 1), :].rearrange(
                    '(c p) f -> p c f', p=128)
                stream_chunk(src,
                             dest[:, 8 * hh:8 * (hh + 1), :].rearrange('p a b -> p (a b)'),
                             True, gate)
        else:
            src_full = w_src(e, 'w2')[fc * 128:(fc + 1) * 128, :]
            for hh in range(2):
                stream_chunk(src_full[:, 1024 * hh:1024 * (hh + 1)],
                             dest[:, 1024 * hh:1024 * (hh + 1)], False, gate)

    wge(sy, SPO)
    wge(sy, SDV)
    wge(ac, SPO)
    wge(ac, SDV)
    wge(ve, SPO)
    for fc0 in range(8):
        emit_wdma(0, fc0, 'w2', rw2b[:, fc0, :])
    w2_ready = (SAC.n, SDV.n)
    emit_wdma(0, 0, 'g', rgb_[0])
    emit_wdma(0, 0, 'w1', rw1b[0])
    w_ready = [(SAC.n, SDV.n), (0, 0)]

    b01_gate = SAC.n
    b23_gate_dve = SDV.n
    b23_gate_ac = cb_gate
    out_war = [(SDV, 0), (SDV, 0)]
    gsl_gate = SDV.n
    rgb_free = [0, 0]
    rw1_free = [0, 0]
    rw2_free = 0

    for e in range(NEXP):
        routed = e < NE
        for fc in range(8):
            wpar = fc % 2
            if fc + 1 < 8:
                p2 = (fc + 1) % 2
                emit_wdma(e, fc + 1, 'g', rgb_[p2], gate=rgb_free[p2])
                emit_wdma(e, fc + 1, 'w1', rw1b[p2], gate=rw1_free[p2])
                w_ready[p2] = (SAC.n, SDV.n)
            elif e + 1 < NEXP:
                emit_wdma(e + 1, 0, 'g', rgb_[0], gate=rgb_free[0])
                emit_wdma(e + 1, 0, 'w1', rw1b[0], gate=rw1_free[0])
                w_ready[0] = (SAC.n, SDV.n)
            wge(pe, SAC, max(b01_gate, w_ready[wpar][0]))
            wge(pe, SDV, w_ready[wpar][1])
            for ic in range(16):
                pe.matmul(bankf(0), rgb_[wpar][:, ic, :], ln2[:, ic, 0:512],
                          start=(ic == 0), stop=(ic == 15))
                i1 = pe.matmul(bankf(1), rgb_[wpar][:, ic, :], ln2[:, ic, 512:1024],
                               start=(ic == 0), stop=(ic == 15))
            SPE.inc(i1, 1)
            rgb_free[wpar] = SPE.n
            wge(ac, SPE)
            wge(ac, SDV, gsl_gate)
            for th in range(2):
                SAC.inc(ac.activation(gsl[wpar][:, th * 512:(th + 1) * 512], bankf(th),
                                      AF.Gelu, bias=gbias(e, fc)), 1)
            b01_gate = SAC.n
            wge(pe, SDV, max(b23_gate_dve, w_ready[wpar][1]))
            wge(pe, SAC, b23_gate_ac)
            for ic in range(16):
                pe.matmul(bankf(2), rw1b[wpar][:, ic, :], ln2[:, ic, 0:512],
                          start=(ic == 0), stop=(ic == 15))
                i1 = pe.matmul(bankf(3), rw1b[wpar][:, ic, :], ln2[:, ic, 512:1024],
                               start=(ic == 0), stop=(ic == 15))
            SPE.inc(i1, 1)
            rw1_free[wpar] = SPE.n
            wge(ve, SPE)
            wge(ve, SAC, b01_gate)
            for th in range(2):
                SDV.inc(ve.scalar_tensor_tensor(hsl[:, fc, th * 512:(th + 1) * 512],
                                                bankf(2 + th), bbias(e, fc),
                                                gsl[wpar][:, th * 512:(th + 1) * 512],
                                                ALU.add, ALU.mult), 1)
            b23_gate_dve = SDV.n
            gsl_gate = SDV.n
            if routed:
                wge(po, SDV, b23_gate_dve)
                wge(po, SAC, b23_gate_ac)
                SPO.inc(po.tensor_tensor(hsl[:, fc, :], hsl[:, fc, :], cbuf[e % 2], ALU.mult), 1)
        h_dve, h_po = b23_gate_dve, SPO.n
        if e + 1 < NE:
            en = e + 1
            wge(pe, SDV, b23_gate_dve)
            for tt in range(8):
                bb = 2 + tt // 4
                ins = pe.matmul(PS[:, bb, (tt % 4) * 128:(tt % 4 + 1) * 128],
                                cbf[:, tt, en:en + 1].broadcast_to([128, 128]), ident,
                                start=True, stop=True)
            SPE.inc(ins, 1)
            wge(ac, SPE)
            wge(ac, SPO, h_po)
            SAC.inc(ac.activation(cbuf[en % 2][:, 0:512], bankf(2), AF.Copy), 1)
            SAC.inc(ac.activation(cbuf[en % 2][:, 512:1024], bankf(3), AF.Copy), 1)
            b23_gate_ac = SAC.n
        wge(pe, SDV, h_dve)
        wge(pe, SPO, h_po)
        wge(pe, SAC, w2_ready[0])
        wge(pe, SDV, w2_ready[1])
        for dc in range(16):
            pr = dc % 2
            bb0, bb1 = (4, 5) if pr == 0 else (6, 7)
            ws, wv = out_war[pr]
            wge(pe, ws, wv)
            for fc in range(8):
                pe.matmul(bankf(bb0), rw2b[:, fc, dc * 128:(dc + 1) * 128],
                          hsl[:, fc, 0:512], start=(fc == 0), stop=(fc == 7))
                i1 = pe.matmul(bankf(bb1), rw2b[:, fc, dc * 128:(dc + 1) * 128],
                               hsl[:, fc, 512:1024], start=(fc == 0), stop=(fc == 7))
            SPE.inc(i1, 1)
            wge(ve, SPE)
            SDV.inc(ve.tensor_tensor(acc[:, dc, :], acc[:, dc, :],
                                     PS[:, bb0:bb0 + 2, :].rearrange('p a b -> p (a b)'),
                                     ALU.add), 1)
            out_war[pr] = (SDV, SDV.n)
        rw2_free = SPE.n
        if e + 1 < NEXP:
            for fc0 in range(8):
                emit_wdma(e + 1, fc0, 'w2', rw2b[:, fc0, :], gate=rw2_free)
            w2_ready = (SAC.n, SDV.n)

    for dc in range(16):
        pr = dc % 2
        bb0, bb1 = (4, 5) if pr == 0 else (6, 7)
        ws, wv = out_war[pr]
        wge(pe, ws, wv)
        wge(pe, SAC)
        for th in range(2):
            bb = bb0 if th == 0 else bb1
            pe.matmul(bankf(bb), rb2b[0:16, dc * 128:(dc + 1) * 128],
                      cfm[0:16, th * 512:(th + 1) * 512], start=True, stop=False)
            ins = pe.matmul(bankf(bb), sb2b[0:2, dc * 128:(dc + 1) * 128],
                            ones_bf[0:2, th * 512:(th + 1) * 512], start=False, stop=True)
        SPE.inc(ins, 1)
        wge(ve, SPE)
        SDV.inc(ve.tensor_tensor(acc[:, dc, :], acc[:, dc, :],
                                 PS[:, bb0:bb0 + 2, :].rearrange('p a b -> p (a b)'),
                                 ALU.add), 1)
        out_war[pr] = (SDV, SDV.n)

    wge(sy, SDV)
    SOUT.inc(sy.dma_start(out=o.rearrange('(c p) t -> p c t', p=128), in_=acc), 16)
    wge(sy, SOUT)
    return nc


def kernel(**inputs):
    f32 = np.float32
    bf = ml_dtypes.bfloat16
    x = np.asarray(inputs['L0_attn_x'], f32)
    Wq = np.asarray(inputs['L0_attn_Wq_w'], f32)
    Wdkv = np.asarray(inputs['L0_attn_Wdkv_w'], f32)
    Wuk = np.asarray(inputs['L0_attn_Wuk_w'], f32)
    Wuv = np.asarray(inputs['L0_attn_Wuv_w'], f32)
    Wo = np.asarray(inputs['L0_attn_Wo_w'], f32)

    def biasT(v, nch):
        E = v.shape[0]
        return np.ascontiguousarray(v.T.reshape(nch, 128, E).transpose(1, 0, 2))

    shared = dict(
        wq=np.ascontiguousarray(Wq),
        wqt=np.ascontiguousarray(Wq.T),
        wuk=np.ascontiguousarray(Wuk),
        wuv=np.ascontiguousarray(Wuv),
        wot=np.ascontiguousarray(Wo.T),
        wdkvt=np.ascontiguousarray(Wdkv.T),
        rg=np.ascontiguousarray(np.asarray(inputs['L0_ffn_rg'], f32)),
        rw1=np.ascontiguousarray(np.asarray(inputs['L0_ffn_rw1'], f32)),
        rw2=np.ascontiguousarray(np.asarray(inputs['L0_ffn_rw2'], f32)),
        sg=np.ascontiguousarray(np.asarray(inputs['L0_ffn_sg'], f32)),
        sw1=np.ascontiguousarray(np.asarray(inputs['L0_ffn_sw1'], f32)),
        sw2=np.ascontiguousarray(np.asarray(inputs['L0_ffn_sw2'], f32)),
        wa=np.ascontiguousarray(np.asarray(inputs['L0_ffn_wa'], f32)),
        rgbT=biasT(np.asarray(inputs['L0_ffn_rgb'], f32), 8),
        rb1T=biasT(np.asarray(inputs['L0_ffn_rb1'], f32), 8),
        sgbT=biasT(np.asarray(inputs['L0_ffn_sgb'], f32), 8),
        sb1T=biasT(np.asarray(inputs['L0_ffn_sb1'], f32), 8),
        rb2=np.ascontiguousarray(np.asarray(inputs['L0_ffn_rb2'], f32)),
        sb2=np.ascontiguousarray(np.asarray(inputs['L0_ffn_sb2'], f32)),
        identD=np.eye(128, dtype=f32).astype(bf),
    )

    xT = [np.ascontiguousarray(x[b].T) for b in range(B)]

    in_maps, tiles = [], []
    for k in range(8):
        b, r = k // 4, k % 4
        Ts = [4 * (c - 1) + ((r + c - 1) % 4) for c in range(1, NSLOT + 1)]
        tiles.append((b, Ts))
        xo = np.ascontiguousarray(
            np.concatenate([xT[b][:, 128 * t:128 * t + 128] for t in Ts], axis=1))
        kk = np.arange(512)[None, :]
        qq = np.arange(128)[:, None]
        mk = np.zeros((128, NSLOT, 512), dtype=f32)
        for s in range(NSLOT):
            m = (r + s) % 4
            mk[:, s, :] = np.where(kk <= 128 * m + qq, 0.0, NEG)
        im = dict(shared)
        im['xall'] = xT[b]
        im['xown'] = xo
        im['masksD'] = mk.astype(bf)
        in_maps.append(im)

    nc = build_program()
    res = run_bass_kernel_spmd(nc, in_maps, core_ids=list(range(8)))

    out = np.empty((B, T, D), dtype=f32)
    for k in range(8):
        b, Ts = tiles[k]
        ok = res.results[k]['o']
        for s, t in enumerate(Ts):
            out[b, 128 * t:128 * t + 128, :] = ok[:, 128 * s:128 * (s + 1)].T
    return out


# version 09
# speedup vs baseline: 1.1601x; 1.1601x over previous
import numpy as np
import ml_dtypes
import concourse.bass as bass
import concourse.mybir as mybir
from concourse.bass_utils import run_bass_kernel_spmd

F32 = mybir.dt.float32
BF16 = mybir.dt.bfloat16
U8 = mybir.dt.uint8
AF = mybir.ActivationFunctionType
ALU = mybir.AluOpType
AX = mybir.AxisListType

NH, HS, NL, D, T, B = 16, 128, 512, 2048, 4096, 2
NTOK = 1024
NSLOT = 8
NE, NS, FF = 16, 2, 1024
NEXP = NE + NS
ISCALE = float(1.0 / np.sqrt(128.0))
NEG = -1.0e30
ARENA = 211968

MAP = {}
MAP['ident'] = 0
MAP['ones_bf'] = 256
MAP['ones_f'] = 2304
MAP['rows'] = 2816
MAP['amisc'] = 11008
MAP['x1'] = 12032
A0 = 44800
MAP['wbig'] = A0
MAP['wsm'] = A0 + 65536
MAP['wst0'] = A0 + 81920
MAP['keffo'] = A0 + 114688
MAP['vmato'] = A0 + 131072
MAP['wqt'] = A0
MAP['xst'] = A0 + 65536
MAP['lng'] = A0 + 98304
MAP['qbuf'] = A0 + 114688
MAP['tmp1a'] = A0 + 122880
MAP['rstd1a'] = A0 + 139264
MAP['keff_sb'] = A0 + 143360
MAP['qlbuf'] = A0 + 159744
MAP['wdkvt'] = A0
MAP['xst_b'] = A0 + 16384
MAP['lng_b'] = A0 + 49152
MAP['tmp1b'] = A0 + 65536
MAP['rstd1b'] = A0 + 81920
MAP['ckv_tm'] = 136192
MAP['ckv_fm'] = 168960
MAP['pbuf'] = A0
MAP['pt'] = A0 + 16384
MAP['qlath'] = A0 + 24576
MAP['vmath'] = A0 + 40960
MAP['masks'] = A0 + 43008
MAP['ylat_all'] = A0 + 51200
MAP['ylatt_all'] = A0 + 67584
MAP['pbuf3'] = A0 + 75776
MAP['rw2'] = 12032
MAP['ln2'] = A0
MAP['hsl'] = A0 + 32768
MAP['rgb_'] = A0 + 49152
MAP['rw1b'] = A0 + 57344
MAP['tmp3'] = A0 + 32768
MAP['gsl'] = A0 + 65536
MAP['cfm'] = A0 + 69632
MAP['cbf'] = A0 + 71680
MAP['cb'] = A0 + 72192
MAP['aff'] = A0 + 76288
MAP['wab'] = A0 + 76800
MAP['rgbT'] = A0 + 77312
MAP['rb1T'] = A0 + 77824
MAP['sgbT'] = A0 + 78336
MAP['sb1T'] = A0 + 78848
MAP['rb2b'] = A0 + 79360
MAP['sb2b'] = A0 + 83456
MAP['sqs'] = A0 + 87552
MAP['tks'] = A0 + 88576
MAP['rstd3'] = A0 + 89088
MAP['stg3'] = A0 + 93184
MAP['acc'] = A0 + 101376
assert MAP['acc'] + 65536 <= ARENA


def build_program():
    nc = bass.Bass("TRN2", target_bir_lowering=False)

    ap_ = lambda h: h.ap()
    xall = ap_(nc.dram_tensor('xall', [D, T], F32, kind='ExternalInput'))
    xown = ap_(nc.dram_tensor('xown', [D, NTOK], F32, kind='ExternalInput'))
    wq = ap_(nc.dram_tensor('wq', [D, D], F32, kind='ExternalInput'))
    wqt = ap_(nc.dram_tensor('wqt', [D, D], F32, kind='ExternalInput'))
    wuk = ap_(nc.dram_tensor('wuk', [D, NL], F32, kind='ExternalInput'))
    wuv = ap_(nc.dram_tensor('wuv', [D, NL], F32, kind='ExternalInput'))
    wot = ap_(nc.dram_tensor('wot', [D, D], F32, kind='ExternalInput'))
    wdkvt = ap_(nc.dram_tensor('wdkvt', [D, NL], F32, kind='ExternalInput'))
    rg = ap_(nc.dram_tensor('rg', [NE, D, FF], F32, kind='ExternalInput'))
    rw1 = ap_(nc.dram_tensor('rw1', [NE, D, FF], F32, kind='ExternalInput'))
    rw2 = ap_(nc.dram_tensor('rw2', [NE, FF, D], F32, kind='ExternalInput'))
    sg = ap_(nc.dram_tensor('sg', [NS, D, FF], F32, kind='ExternalInput'))
    sw1 = ap_(nc.dram_tensor('sw1', [NS, D, FF], F32, kind='ExternalInput'))
    sw2 = ap_(nc.dram_tensor('sw2', [NS, FF, D], F32, kind='ExternalInput'))
    wa = ap_(nc.dram_tensor('wa', [D, NE], F32, kind='ExternalInput'))
    rgbT = ap_(nc.dram_tensor('rgbT', [128, 8, NE], F32, kind='ExternalInput'))
    rb1T = ap_(nc.dram_tensor('rb1T', [128, 8, NE], F32, kind='ExternalInput'))
    sgbT = ap_(nc.dram_tensor('sgbT', [128, 8, NS], F32, kind='ExternalInput'))
    sb1T = ap_(nc.dram_tensor('sb1T', [128, 8, NS], F32, kind='ExternalInput'))
    rb2 = ap_(nc.dram_tensor('rb2', [NE, D], F32, kind='ExternalInput'))
    sb2 = ap_(nc.dram_tensor('sb2', [NS, D], F32, kind='ExternalInput'))
    identD = ap_(nc.dram_tensor('identD', [128, 128], BF16, kind='ExternalInput'))
    masksD = ap_(nc.dram_tensor('masksD', [128, 8, 512], BF16, kind='ExternalInput'))
    o = ap_(nc.dram_tensor('o', [D, NTOK], F32, kind='ExternalOutput'))
    qlscr = ap_(nc.dram_tensor('qlscr', [NH, 128, 4, NTOK], BF16, kind='Internal'))
    kescr = ap_(nc.dram_tensor('kescr', [128, 16, 512], BF16, kind='Internal'))
    vmscr = ap_(nc.dram_tensor('vmscr', [128, 4, 2048], BF16, kind='Internal'))

    arena = nc.alloc_sbuf_tensor('arena', [128, ARENA], U8)
    AR = arena.ap()

    def carve(name, shape, dtype):
        off = MAP[name]
        es = mybir.dt.size(dtype)
        n = int(np.prod(shape[1:])) * es
        v = AR[:, off:off + n].bitcast(dtype)
        if len(shape) == 3:
            v = v.rearrange('p (a b) -> p a b', a=shape[1])
        return v

    def carve_at(off, nbytes, dtype):
        return AR[:, off:off + nbytes].bitcast(dtype)

    pball = nc.alloc_psum_tensor('pball', [128, 8, 512], F32)
    PS = pball.ap()
    bankbf = lambda i: PS[:, i, :].bitcast(BF16)
    bankpair = lambda i: PS[:, i:i + 2, :].rearrange('p a b -> p (a b)')

    def bankf(i, lo=0, hi=512):
        return PS[:, i, lo:hi]

    class Sem:
        __slots__ = ('h', 'n')
        def __init__(self, name):
            self.h = nc.alloc_semaphore(name)
            self.n = 0
        def inc(self, instr, k=1):
            instr.then_inc(self.h, k)
            self.n += k
            return instr

    SW = Sem('sw'); SQO = Sem('sqo')
    SXp = [Sem('sx0'), Sem('sx1')]
    SQLp = [Sem('sql0'), Sem('sql1')]
    SWp = [Sem('sw0'), Sem('sw1')]
    SWs = [Sem('sws0'), Sem('sws1'), Sem('sws2')]
    SQIp = [Sem('sqi0'), Sem('sqi1')]
    SSM = Sem('ssm'); SOUT = Sem('sout')
    SPE = Sem('spe'); SAC = Sem('sac'); SDV = Sem('sdv'); SPO = Sem('spo')

    pe, ve, ac, po, sy = nc.tensor, nc.vector, nc.scalar, nc.gpsimd, nc.sync

    lastw = {}

    def wge(eng, s, val=None):
        v = s.n if val is None else val
        if v <= 0:
            return
        k = (id(eng), id(s))
        if lastw.get(k, -1) >= v:
            return
        eng.wait_ge(s.h, v)
        lastw[k] = v

    def dve_edge():
        wge(ve, SDV)

    def ac_edge():
        wge(ac, SAC)

    for s in (SW, SXp[0], SXp[1], SQLp[0], SQLp[1], SWp[0], SWp[1], SWs[0], SWs[1], SWs[2],
              SQIp[0], SQIp[1], SQO, SSM, SOUT, SPE, SAC, SDV, SPO):
        po.sem_clear(s.h)
    nc.all_engine_barrier()

    ident = carve('ident', [128, 128], BF16)
    ones_bf = carve('ones_bf', [128, 1024], BF16)
    ones_f = carve('ones_f', [128, 128], F32)
    rows = carve('rows', [128, 2048], F32)
    amisc = carve('amisc', [128, 256], F32)
    sqpp = [carve_at(MAP['amisc'], 512, BF16), carve_at(MAP['amisc'] + 512, 512, BF16)]
    x1 = carve('x1', [128, 16, 1024], BF16)

    SSM.inc(sy.dma_start(out=ident, in_=identD), 16)
    SSM.inc(po.memset(ones_bf, 1.0), 1)
    SSM.inc(po.memset(ones_f, 1.0), 1)

    lw_last = {}

    def load_weight_bf(src, dest_bf, nrow_chunks, cols, stage_off, stage_cap):
        per = max(1, stage_cap // cols)
        stg = [carve_at(stage_off + i * stage_cap * 4, stage_cap * 4, F32) for i in (0, 1)]
        done, i, pend = 0, 0, []
        src3 = src.rearrange('(c p) f -> p c f', p=128)
        while done < nrow_chunks or pend:
            while done < nrow_chunks and len(pend) < 2:
                m = min(per, nrow_chunks - done)
                par = i % 2
                dst = stg[par][:, 0:m * cols].rearrange('p (c f) -> p c f', c=m)
                last = lw_last.get((stage_off, par))
                if last is not None:
                    wge(sy, last[0], last[1])
                SWp[par].inc(sy.dma_start(out=dst, in_=src3[:, done:done + m, :]), 16)
                pend.append((par, done, m, SWp[par].n))
                done += m
                i += 1
            sidx, st_c, m, swn = pend.pop(0)
            sv = stg[sidx][:, 0:m * cols]
            dv = dest_bf[:, st_c:st_c + m, :].rearrange('p a b -> p (a b)')
            if sidx == 0:
                wge(ac, SWp[sidx], swn)
                SAC.inc(ac.activation(dv, sv, AF.Copy), 1)
                lw_last[(stage_off, sidx)] = (SAC, SAC.n)
            else:
                wge(ve, SWp[sidx], swn)
                SDV.inc(ve.tensor_copy(dv, sv), 1)
                lw_last[(stage_off, sidx)] = (SDV, SDV.n)

    lnsq_gate = [0, 0]
    ln_rows_free = [0]
    ln_apply_po = [0, 0]
    ln_apply_ve = [0, 0]
    ln_tmp_free = [0]
    ln_bcast = [0, 0]

    def ln_S(x_f32, ntok, rstd_sb, mrs_sb, par):
        wge(pe, SXp[par])
        wge(ac, SXp[par])
        wge(pe, SAC)
        for c in range(16):
            ins = pe.matmul(PS[0:1, 0, 0:ntok], ones_f[:, 0:1], x_f32[:, c, :],
                            start=(c == 0), stop=(c == 15))
        SPE.inc(ins, 1)
        for c in range(16):
            sq = sqpp[c % 2]
            wge(ac, SPE, lnsq_gate[c % 2])
            SAC.inc(ac.activation(sq[:, 0:ntok], x_f32[:, c, :], AF.Square), 1)
            wge(pe, SAC)
            ins = pe.matmul(PS[0:1, 1, 0:ntok], ones_bf[:, 0:1], sq[:, 0:ntok],
                            start=(c == 0), stop=(c == 15))
            SPE.inc(ins, 1)
            lnsq_gate[c % 2] = SPE.n
        mean = rows[0:1, 0:ntok]
        msq = rows[0:1, 512:512 + ntok]
        mean2 = rows[0:1, 1024:1024 + ntok]
        rstd = rows[0:1, 1536:1536 + ntok]
        wge(ac, SPE)
        wge(ac, SDV, ln_rows_free[0])
        SAC.inc(ac.activation(mean, PS[0:1, 0, 0:ntok], AF.Copy, scale=1.0 / D), 1)
        SAC.inc(ac.activation(msq, PS[0:1, 1, 0:ntok], AF.Copy, scale=1.0 / D), 1)
        ac_edge()
        SAC.inc(ac.activation(mean2, mean, AF.Square), 1)
        wge(ve, SAC)
        SDV.inc(ve.scalar_tensor_tensor(msq, msq, 1.0e-5, mean2,
                                        ALU.add, ALU.subtract), 1)
        wge(ac, SDV)
        SAC.inc(ac.activation(mean2, msq, AF.Ln), 1)
        ac_edge()
        SAC.inc(ac.activation(rstd, mean2, AF.Exp, scale=-0.5), 1)
        ac_edge()
        wge(ve, SAC)
        SDV.inc(ve.tensor_mul(mean2, mean, rstd), 1)
        ln_rows_free[0] = SDV.n
        wge(pe, SAC)
        wge(pe, SDV)
        SPE.inc(pe.matmul(PS[:, 6, 0:ntok], ones_f[0:1, :], rstd, start=True, stop=True), 1)
        SPE.inc(pe.matmul(PS[:, 7, 0:ntok], ones_f[0:1, :], mean2, start=True, stop=True), 1)
        wge(ac, SPE)
        wge(ac, SPO, ln_apply_po[par])
        wge(ac, SDV, ln_apply_ve[par])
        SAC.inc(ac.activation(rstd_sb[:, 0:ntok], PS[:, 6, 0:ntok], AF.Copy), 1)
        SAC.inc(ac.activation(mrs_sb[:, 0:ntok], PS[:, 7, 0:ntok], AF.Copy), 1)
        ln_bcast[par] = SAC.n

    def ln_T(x_f32, ln_out, ntok, rstd_sb, mrs_sb, tmp_big, par, lng_free_val):
        wge(po, SAC, ln_bcast[par])
        wge(po, SDV, ln_tmp_free[0])
        for c in range(16):
            ins = po.tensor_mul(tmp_big[:, c, 0:ntok], x_f32[:, c, :], rstd_sb[:, 0:ntok])
        SPO.inc(ins, 1)
        ln_apply_po[par] = SPO.n
        wge(ve, SPO)
        wge(ve, SPE, lng_free_val)
        for c in range(16):
            ins = ve.tensor_sub(ln_out[:, c, :], tmp_big[:, c, 0:ntok], mrs_sb[:, 0:ntok])
        SDV.inc(ins, 1)
        ln_apply_ve[par] = SDV.n
        ln_tmp_free[0] = SDV.n

    wbig = carve('wbig', [128, 16, 2048], BF16)
    wsm = carve('wsm', [128, 16, 512], BF16)
    keffo = carve('keffo', [128, 16, 512], BF16)
    vmato = carve('vmato', [128, 4, 2048], BF16)

    load_weight_bf(wq, wbig, 16, 2048, MAP['wst0'], 4096)
    load_weight_bf(wuk, wsm, 16, 512, MAP['wst0'], 4096)
    wge(pe, SAC)
    wge(pe, SDV)
    for pc in range(16):
        b = pc % 2
        wge(pe, SAC)
        for jc in range(16):
            ins = pe.matmul(bankf(b), wbig[:, jc, pc * 128:(pc + 1) * 128], wsm[:, jc, :],
                            start=(jc == 0), stop=(jc == 15))
        SPE.inc(ins, 1)
        wge(ac, SPE)
        SAC.inc(ac.activation(keffo[:, pc, :], bankf(b), AF.Copy), 1)
    wge(sy, SAC)
    SQO.inc(sy.dma_start(out=kescr, in_=keffo), 16)

    load_weight_bf(wot, wbig, 16, 2048, MAP['wst0'], 4096)
    load_weight_bf(wuv, wsm, 16, 512, MAP['wst0'], 4096)
    wge(pe, SAC)
    wge(pe, SDV)
    for lc in range(4):
        for ds in range(4):
            b = 2 + (lc * 4 + ds) % 2
            wge(pe, SAC)
            for mc in range(16):
                ins = pe.matmul(bankf(b), wsm[:, mc, lc * 128:(lc + 1) * 128],
                                wbig[:, mc, ds * 512:(ds + 1) * 512],
                                start=(mc == 0), stop=(mc == 15))
            SPE.inc(ins, 1)
            wge(ac, SPE)
            SAC.inc(ac.activation(vmato[:, lc, ds * 512:(ds + 1) * 512], bankf(b), AF.Copy), 1)
    wge(sy, SAC)
    SQO.inc(sy.dma_start(out=vmscr, in_=vmato), 16)

    nc.all_engine_barrier()

    wqt_sb = carve('wqt', [128, 16, 2048], BF16)
    load_weight_bf(wqt, wqt_sb, 16, 2048, MAP['wst0'], 4096)
    nc.all_engine_barrier()
    keff_sb = carve('keff_sb', [128, 16, 512], BF16)
    qlbuf = [carve_at(MAP['qlbuf'] + i * 2048, 2048, BF16) for i in (0, 1)]
    wge(sy, SQO)
    SSM.inc(sy.dma_start(out=keff_sb, in_=kescr), 16)
    wge(po, SQO)
    wge(ac, SQO)

    xst = [carve_at(MAP['xst'] + i * 16384, 16384, F32).rearrange('p (a b) -> p a b', a=16)
           for i in (0, 1)]
    lng = [carve_at(MAP['lng'] + i * 8192, 8192, BF16).rearrange('p (a b) -> p a b', a=16)
           for i in (0, 1)]
    qbuf = carve('qbuf', [128, 16, 256], BF16)
    tmp1a = carve('tmp1a', [128, 16, 256], F32)
    rstd1a = [carve_at(MAP['rstd1a'] + i * 2048, 1024, F32) for i in (0, 1)]
    mrs1a = [carve_at(MAP['rstd1a'] + i * 2048 + 1024, 1024, F32) for i in (0, 1)]

    cons_free = {2: 0, 3: 0, 4: 0, 5: 0}
    lng_free = [0, 0]
    sx_done = {}
    qb_free = [0]
    ql_free = [0]

    def stage1a_T(g):
        par = g % 2
        wge(po, SXp[g % 2], sx_done[g])
        SPO.inc(po.tensor_copy(x1[:, :, g * 256:(g + 1) * 256], xst[par]), 1)
        ln_T(xst[par], lng[par], 256, rstd1a[par], mrs1a[par], tmp1a, par,
             lng_free[par])
        wge(pe, SDV, ln_apply_ve[par])
        for jc in range(16):
            b = 2 + jc % 2
            wge(pe, SAC, cons_free[b])
            for ic in range(16):
                ins = pe.matmul(bankf(b, 0, 256), wqt_sb[:, ic, jc * 128:(jc + 1) * 128],
                                lng[par][:, ic, :], start=(ic == 0), stop=(ic == 15))
            SPE.inc(ins, 1)
            wge(ac, SPE)
            wge(ac, SPE, qb_free[0])
            SAC.inc(ac.activation(qbuf[:, jc, :], bankf(b, 0, 256), AF.Copy), 1)
            cons_free[b] = SAC.n
        lng_free[par] = SPE.n
        qv = SAC.n
        wge(pe, SSM)
        wge(pe, SAC, qv)
        for hh in range(16):
            wge(pe, SDV, ql_free[0])
            for lc in range(4):
                ins = pe.matmul(PS[:, 4 + lc // 2, (lc % 2) * 256:(lc % 2) * 256 + 256],
                                keff_sb[:, hh, lc * 128:(lc + 1) * 128],
                                qbuf[:, hh, :], start=True, stop=True)
            SPE.inc(ins, 1)
            wge(ve, SPE)
            wge(ve, SQLp[hh % 2])
            SDV.inc(ve.tensor_scalar(qlbuf[hh % 2][:, 0:512],
                                     PS[:, 4, :], ISCALE, None, ALU.mult), 1)
            SDV.inc(ve.tensor_scalar(qlbuf[hh % 2][:, 512:1024],
                                     PS[:, 5, :], ISCALE, None, ALU.mult), 1)
            ql_free[0] = SDV.n
            wge(sy, SDV, ql_free[0])
            SQLp[hh % 2].inc(
                sy.dma_start(out=qlscr[hh][:, :, g * 256:(g + 1) * 256],
                             in_=qlbuf[hh % 2].rearrange('p (a b) -> p a b', a=4)), 16)
        qb_free[0] = SPE.n

    for g in range(4):
        wge(sy, SPO)
        SXp[g % 2].inc(sy.dma_start(out=xst[g % 2],
                            in_=xown[:, g * 256:(g + 1) * 256].rearrange('(c p) t -> p c t', p=128)), 16)
        sx_done[g] = SXp[g % 2].n
        ln_S(xst[g % 2], 256, rstd1a[g % 2], mrs1a[g % 2], g % 2)
        if g >= 1:
            stage1a_T(g - 1)
    stage1a_T(3)

    nc.all_engine_barrier()

    wdkvt_sb = carve('wdkvt', [128, 16, 512], BF16)
    load_weight_bf(wdkvt, wdkvt_sb, 16, 512, MAP['xst_b'], 4096)
    nc.all_engine_barrier()
    wge(ac, SQO)

    xstb = [carve_at(MAP['xst_b'] + i * 16384, 16384, F32).rearrange('p (a b) -> p a b', a=16)
            for i in (0, 1)]
    lngb = [carve_at(MAP['lng_b'] + i * 8192, 8192, BF16).rearrange('p (a b) -> p a b', a=16)
            for i in (0, 1)]
    tmp1b = carve('tmp1b', [128, 16, 256], F32)
    rstd1b = [carve_at(MAP['rstd1b'] + i * 2048, 1024, F32) for i in (0, 1)]
    mrs1b = [carve_at(MAP['rstd1b'] + i * 2048 + 1024, 1024, F32) for i in (0, 1)]
    ckv_tm = carve('ckv_tm', [128, 32, 512], BF16)
    ckv_fm = carve('ckv_fm', [128, 4, 4096], BF16)

    def stage1b_T(g):
        par = g % 2
        ln_T(xstb[par], lngb[par], 256, rstd1b[par], mrs1b[par], tmp1b, par,
             lng_free[par])
        wge(pe, SDV, ln_apply_ve[par])
        for kt in range(2):
            b = 2 + kt
            wge(pe, SAC, cons_free[b])
            for ic in range(16):
                ins = pe.matmul(bankf(b), lngb[par][:, ic, kt * 128:(kt + 1) * 128],
                                wdkvt_sb[:, ic, :], start=(ic == 0), stop=(ic == 15))
            SPE.inc(ins, 1)
            wge(ac, SPE)
            SAC.inc(ac.activation(ckv_tm[:, g * 2 + kt, :], bankf(b), AF.Copy), 1)
            cons_free[b] = SAC.n
        for lc in range(4):
            b = 4 + lc % 2
            wge(pe, SAC, cons_free[b])
            for ic in range(16):
                ins = pe.matmul(bankf(b, 0, 256), wdkvt_sb[:, ic, lc * 128:(lc + 1) * 128],
                                lngb[par][:, ic, :], start=(ic == 0), stop=(ic == 15))
            SPE.inc(ins, 1)
            wge(ac, SPE)
            SAC.inc(ac.activation(ckv_fm[:, lc, g * 256:(g + 1) * 256], bankf(b, 0, 256), AF.Copy), 1)
            cons_free[b] = SAC.n
        lng_free[par] = SPE.n

    for g in range(16):
        wge(sy, SPO)
        SXp[g % 2].inc(sy.dma_start(out=xstb[g % 2],
                            in_=xall[:, g * 256:(g + 1) * 256].rearrange('(c p) t -> p c t', p=128)), 16)
        ln_S(xstb[g % 2], 256, rstd1b[g % 2], mrs1b[g % 2], g % 2)
        if g >= 1:
            stage1b_T(g - 1)
    stage1b_T(15)

    nc.all_engine_barrier()

    pbuf = [carve_at(MAP['pbuf'] + i * 8192, 8192, BF16) for i in (0, 1)] + \
           [carve_at(MAP['pbuf3'], 8192, BF16)]
    ptb = carve('pt', [128, 32, 128], BF16)
    qlath = [carve_at(MAP['qlath'] + i * 8192, 8192, BF16).rearrange('p (a b) -> p a b', a=4)
             for i in (0, 1)]
    vmath = [carve_at(MAP['vmath'] + i * 1024, 1024, BF16).rearrange('p (a b) -> p a b', a=4)
             for i in (0, 1)]
    ylat_all = [carve_at(MAP['ylat_all'] + i * 8192, 8192, BF16).rearrange('p (a b) -> p a b', a=8)
                for i in (0, 1)]
    ylatt_all = carve('ylatt_all', [128, 4, 1024], BF16)
    masks = carve('masks', [128, 8, 512], BF16)
    nmx = [amisc[:, 80 * p:80 * p + 16] for p in range(3)]
    sums = [amisc[:, 80 * p + 16:80 * p + 32] for p in range(3)]
    rr = [amisc[:, 80 * p + 32:80 * p + 48] for p in range(3)]
    spr = [amisc[:, 80 * p + 48:80 * p + 64] for p in range(3)]
    negm = [amisc[:, 80 * p + 64:80 * p + 65] for p in range(3)]
    stot = [amisc[:, 80 * p + 65:80 * p + 66] for p in range(3)]
    recip = [amisc[:, 80 * p + 66:80 * p + 67] for p in range(3)]

    wge(sy, SQO)
    wge(sy, SQLp[0])
    wge(sy, SQLp[1])
    SSM.inc(sy.dma_start(out=masks, in_=masksD), 16)

    qh_free = [0, 0]

    def load_head(h):
        par = h % 2
        wge(sy, SPE, qh_free[par])
        SQIp[par].inc(sy.dma_start(out=qlath[par], in_=qlscr[h]), 16)
        SQIp[par].inc(sy.dma_start(out=vmath[par], in_=vmscr[:, :, h * 128:(h + 1) * 128]), 16)

    load_head(0)
    wge(pe, SSM)
    snap_p_free = [0, 0, 0]
    snap_sb = [0, 0, 0]
    nm_free = [0, 0, 0]
    tb_free = {3: (SDV, 0), 7: (SDV, 0)}
    b45_free = {4: 0, 5: 0}


    def attn_a(it, h, si):
        c = si + 1
        hp = h % 2
        par = it % 3
        if si == 0:
            wge(pe, SQIp[hp], 32 * (h // 2 + 1))
        if si == 3 and h + 1 < NH:
            load_head(h + 1)
        wge(ve, SAC, nm_free[par])
        for kc in range(c):
            b = kc % 3
            wge(pe, SAC, snap_sb[b])
            for lc in range(4):
                ins = pe.matmul(bankf(b), qlath[hp][:, lc, si * 128:(si + 1) * 128],
                                ckv_fm[:, lc, kc * 512:(kc + 1) * 512],
                                start=(lc == 0), stop=(lc == 3))
            SPE.inc(ins, 1)
            wge(ve, SPE)
            if kc == c - 1:
                SDV.inc(ve.tensor_tensor(bankf(b), bankf(b), masks[:, si, :], ALU.add), 1)
                dve_edge()
            SDV.inc(ve.reduce_max(nmx[par][:, kc:kc + 1], bankf(b), axis=AX.X, negate=True), 1)
            wge(ac, SDV)
            wge(ac, SPE, snap_p_free[par])
            SAC.inc(ac.activation(pbuf[par][:, kc * 512:(kc + 1) * 512], bankf(b), AF.Exp,
                                  bias=nmx[par][:, kc:kc + 1],
                                  accum_out=sums[par][:, kc:kc + 1]), 1)
            snap_sb[b] = SAC.n
        wge(ve, SAC)
        SDV.inc(ve.tensor_reduce(negm[par], nmx[par][:, 0:c], AX.X, ALU.min), 1)
        dve_edge()
        wge(ac, SDV)
        SAC.inc(ac.activation(rr[par][:, 0:c], nmx[par][:, 0:c], AF.Exp,
                              bias=negm[par], scale=-1.0), 1)
        nm_free[par] = SAC.n
        wge(ve, SAC)
        SDV.inc(ve.tensor_mul(spr[par][:, 0:c], sums[par][:, 0:c], rr[par][:, 0:c]), 1)
        dve_edge()
        SDV.inc(ve.reduce_sum(stot[par], spr[par][:, 0:c], axis=AX.X), 1)
        dve_edge()
        SDV.inc(ve.reciprocal(recip[par], stot[par]), 1)
        for kc in range(c):
            SDV.inc(ve.tensor_scalar(pbuf[par][:, kc * 512:(kc + 1) * 512],
                                     pbuf[par][:, kc * 512:(kc + 1) * 512],
                                     rr[par][:, kc:kc + 1], None, ALU.mult), 1)
        return dict(c=c, hp=hp, par=par, h=h, si=si, it=it, p_ready=SDV.n)

    def attn_b(st):
        c, hp, par, h, si = st['c'], st['hp'], st['par'], st['h'], st['si']
        wge(pe, SDV, st['p_ready'])
        for j4 in range(c):
            bb = 3 if j4 % 2 == 0 else 7
            wge(pe, tb_free[bb][0], tb_free[bb][1])
            for jj in range(4):
                ins = pe.transpose(bankbf(bb)[:, jj * 128:(jj + 1) * 128],
                                   pbuf[par][:, (j4 * 4 + jj) * 128:(j4 * 4 + jj + 1) * 128],
                                   ident)
            SPE.inc(ins, 1)
            wge(ve, SPE)
            SDV.inc(ve.tensor_copy(
                ptb[:, j4 * 4:(j4 + 1) * 4, :].rearrange('p a b -> p (a b)'),
                bankbf(bb)[:, 0:512]), 1)
            tb_free[bb] = (SDV, SDV.n)
        snap_p_free[par] = SPE.n
        pvb = 4 + st['it'] % 2
        wge(pe, tb_free[3][0], tb_free[3][1])
        wge(pe, tb_free[7][0], tb_free[7][1])
        wge(pe, SDV, b45_free[pvb])
        for j in range(4 * c):
            ins = pe.matmul(bankf(pvb), ptb[:, j, :], ckv_tm[:, j, :],
                            start=(j == 0), stop=(j == 4 * c - 1))
        SPE.inc(ins, 1)
        wge(ve, SPE)
        SDV.inc(ve.tensor_scalar(ylat_all[hp][:, si, :], bankf(pvb),
                                 recip[par][:, 0:1], None, ALU.mult), 1)
        b45_free[pvb] = SDV.n

    def attn_epi(h):
        hp = h % 2
        ya = max(b45_free[4], b45_free[5])
        for lc in range(4):
            bb = 3 if lc % 2 == 0 else 7
            wge(pe, SDV, ya)
            wge(pe, tb_free[bb][0], tb_free[bb][1])
            for si in range(8):
                ins = pe.transpose(bankbf(bb)[:, si * 128:(si + 1) * 128],
                                   ylat_all[h % 2][:, si, lc * 128:(lc + 1) * 128], ident)
            SPE.inc(ins, 1)
            wge(ac, SPE)
            SAC.inc(ac.activation(ylatt_all[:, lc, :], bankbf(bb)[:, 0:1024],
                                  AF.Copy), 1)
            tb_free[bb] = (SAC, SAC.n)
        ylt = SAC.n
        for th in range(2):
            bb = 4 + th
            wge(pe, SAC, ylt)
            wge(pe, SDV, b45_free[bb])
            for lc in range(4):
                ins = pe.matmul(PS[:, bb, :], vmath[hp][:, lc, :],
                                ylatt_all[:, lc, th * 512:(th + 1) * 512],
                                start=(lc == 0), stop=(lc == 3))
            SPE.inc(ins, 1)
            wge(ve, SPE)
            SDV.inc(ve.tensor_tensor(x1[:, h, th * 512:(th + 1) * 512],
                                     x1[:, h, th * 512:(th + 1) * 512],
                                     PS[:, bb, :], ALU.add), 1)
            b45_free[bb] = SDV.n
        qh_free[hp] = SPE.n

    slots = [(h, si) for h in range(NH) for si in range(NSLOT)]
    pend2 = []
    epi_pend = []
    for it, (h, si) in enumerate(slots):
        if len(pend2) >= 3:
            stb = pend2.pop(0)
            attn_b(stb)
            if epi_pend:
                attn_epi(epi_pend.pop(0))
            if stb['si'] == NSLOT - 1:
                epi_pend.append(stb['h'])
        pend2.append(attn_a(it, h, si))
    for stb in pend2:
        attn_b(stb)
        if epi_pend:
            attn_epi(epi_pend.pop(0))
        if stb['si'] == NSLOT - 1:
            epi_pend.append(stb['h'])
    for hh2 in epi_pend:
        attn_epi(hh2)

    nc.all_engine_barrier()

    ln2 = carve('ln2', [128, 16, 1024], BF16)
    hsl = carve('hsl', [128, 8, 1024], BF16)
    rgb_ = [carve_at(MAP['rgb_'] + i * 4096, 4096, BF16).rearrange('p (a b) -> p a b', a=16)
            for i in (0, 1)]
    rw1b = [carve_at(MAP['rw1b'] + i * 4096, 4096, BF16).rearrange('p (a b) -> p a b', a=16)
            for i in (0, 1)]
    rw2b = carve('rw2', [128, 8, 2048], BF16)
    gsl = [carve_at(MAP['gsl'] + i * 2048, 2048, BF16) for i in (0, 1)]
    cfm = carve('cfm', [128, 1024], BF16)
    cbf = carve('cbf', [128, 8, 16], BF16)
    cbuf = [carve_at(MAP['cb'] + i * 2048, 2048, BF16) for i in (0, 1)]
    aff = carve('aff', [128, 8, 16], F32)
    wab = carve('wab', [128, 16, 16], BF16)
    rgbT_sb = carve('rgbT', [128, 8, 16], F32)
    rb1T_sb = carve('rb1T', [128, 8, 16], F32)
    sgbT_sb = carve('sgbT', [128, 8, 2], F32)
    sb1T_sb = carve('sb1T', [128, 8, 2], F32)
    rb2b = carve('rb2b', [128, 2048], BF16)
    sb2b = carve('sb2b', [128, 2048], BF16)
    sqs = carve('sqs', [128, 512], BF16)
    tks = carve('tks', [128, 64], F32)
    rstd3 = carve_at(MAP['rstd3'], 2048, F32)
    mrs3 = carve_at(MAP['rstd3'] + 2048, 2048, F32)
    acc = carve('acc', [128, 16, 1024], F32)
    tmp3 = carve('tmp3', [128, 16, 512], F32)

    for dst, srcD in ((rgbT_sb, rgbT), (rb1T_sb, rb1T), (sgbT_sb, sgbT), (sb1T_sb, sb1T)):
        SSM.inc(sy.dma_start(out=dst, in_=srcD), 16)
    wa_stage = rows[:, 0:256].rearrange('p (a b) -> p a b', a=16)
    SSM.inc(sy.dma_start(out=wa_stage, in_=wa.rearrange('(c p) e -> p c e', p=128)), 16)
    wge(po, SSM)
    SPO.inc(po.tensor_copy(wab, wa_stage), 1)
    wge(sy, SPO)
    SSM.inc(sy.dma_start(out=rows[0:16, 0:2048], in_=rb2), 16)
    wge(po, SSM)
    SPO.inc(po.tensor_copy(rb2b[0:16, :], rows[0:16, 0:2048]), 1)
    wge(sy, SPO)
    SSM.inc(sy.dma_start(out=rows[0:2, 0:2048], in_=sb2), 16)
    wge(po, SSM)
    SPO.inc(po.tensor_copy(sb2b[0:2, :], rows[0:2, 0:2048]), 1)

    wge(ac, SPO)
    for g in range(2):
        gs = slice(g * 512, (g + 1) * 512)
        wge(pe, SAC)
        for cch in range(16):
            wge(ac, SPE)
            SAC.inc(ac.activation(sqs, x1[:, cch, gs], AF.Square), 1)
            wge(pe, SAC)
            ins0 = pe.matmul(PS[0:1, 0, :], ones_bf[:, 0:1], x1[:, cch, gs],
                             start=(cch == 0), stop=(cch == 15))
            ins1 = pe.matmul(PS[0:1, 1, :], ones_bf[:, 0:1], sqs,
                             start=(cch == 0), stop=(cch == 15))
            SPE.inc(ins1, 1)
        mean = rows[0:1, 0:512]
        msq = rows[0:1, 512:1024]
        mean2 = rows[0:1, 1024:1536]
        rstd = rows[0:1, 1536:2048]
        wge(ac, SPE)
        SAC.inc(ac.activation(mean, PS[0:1, 0, :], AF.Copy, scale=1.0 / D), 1)
        SAC.inc(ac.activation(msq, PS[0:1, 1, :], AF.Copy, scale=1.0 / D), 1)
        ac_edge()
        SAC.inc(ac.activation(mean2, mean, AF.Square), 1)
        wge(ve, SAC)
        SDV.inc(ve.scalar_tensor_tensor(msq, msq, 1.0e-5, mean2,
                                        ALU.add, ALU.subtract), 1)
        wge(ac, SDV)
        SAC.inc(ac.activation(mean2, msq, AF.Ln), 1)
        ac_edge()
        SAC.inc(ac.activation(rstd, mean2, AF.Exp, scale=-0.5), 1)
        ac_edge()
        wge(ve, SAC)
        SDV.inc(ve.tensor_mul(mean2, mean, rstd), 1)
        wge(pe, SAC)
        wge(pe, SDV)
        SPE.inc(pe.matmul(PS[:, 2, :], ones_f[0:1, :], rstd, start=True, stop=True), 1)
        SPE.inc(pe.matmul(PS[:, 3, :], ones_f[0:1, :], mean2, start=True, stop=True), 1)
        wge(ac, SPE)
        SAC.inc(ac.activation(rstd3[:, 0:512], PS[:, 2, :], AF.Copy), 1)
        SAC.inc(ac.activation(mrs3[:, 0:512], PS[:, 3, :], AF.Copy), 1)
        wge(po, SAC)
        for cch in range(16):
            ins = po.tensor_mul(tmp3[:, cch, :], x1[:, cch, gs], rstd3[:, 0:512])
        SPO.inc(ins, 1)
        wge(ve, SPO)
        for cch in range(16):
            ins = ve.tensor_sub(ln2[:, cch, gs], tmp3[:, cch, :], mrs3[:, 0:512])
        SDV.inc(ins, 1)
        wge(po, SDV)
        SPO.inc(po.tensor_tensor(acc[:, :, gs], x1[:, :, gs], ln2[:, :, gs], ALU.add), 1)

    wge(pe, SDV)
    wge(pe, SPO)
    for tt in range(8):
        b = 4 + tt % 2
        wge(pe, SAC)
        wge(pe, SDV)
        for ic in range(16):
            ins = pe.matmul(PS[0:128, b, 0:16], ln2[:, ic, tt * 128:(tt + 1) * 128],
                            wab[:, ic, :], start=(ic == 0), stop=(ic == 15))
        SPE.inc(ins, 1)
        wge(ac, SPE)
        SAC.inc(ac.activation(aff[:, tt, :], PS[0:128, b, 0:16], AF.Sigmoid), 1)
        a_t = aff[:, tt, :]
        t16a, t16b = tks[:, 16:32], tks[:, 32:48]
        m1, m2, den, rp, p1, p2 = (tks[:, 0:1], tks[:, 1:2], tks[:, 2:3],
                                   tks[:, 3:4], tks[:, 4:5], tks[:, 5:6])
        wge(ve, SAC)
        SDV.inc(ve.reduce_max(m1, a_t, axis=AX.X), 1)
        dve_edge()
        SDV.inc(ve.tensor_scalar(t16a, a_t, m1, None, ALU.is_ge), 1)
        dve_edge()
        SDV.inc(ve.scalar_tensor_tensor(t16b, t16a, NEG, a_t, ALU.mult, ALU.add), 1)
        dve_edge()
        SDV.inc(ve.reduce_max(m2, t16b, axis=AX.X), 1)
        dve_edge()
        SDV.inc(ve.tensor_scalar(t16b, a_t, m2, None, ALU.is_ge), 1)
        dve_edge()
        SDV.inc(ve.tensor_sub(t16b, t16b, t16a), 1)
        SDV.inc(ve.tensor_add(den, m1, m2), 1)
        dve_edge()
        SDV.inc(ve.reciprocal(rp, den), 1)
        dve_edge()
        SDV.inc(ve.tensor_mul(p1, m1, rp), 1)
        SDV.inc(ve.tensor_mul(p2, m2, rp), 1)
        dve_edge()
        SDV.inc(ve.tensor_scalar(t16a, t16a, p1, None, ALU.mult), 1)
        SDV.inc(ve.tensor_scalar(t16b, t16b, p2, None, ALU.mult), 1)
        dve_edge()
        SDV.inc(ve.tensor_add(a_t, t16a, t16b), 1)
        dve_edge()
        SDV.inc(ve.tensor_copy(cbf[:, tt, :], a_t), 1)
    wge(pe, SDV)
    wge(pe, SAC)
    for tt in range(8):
        ins = pe.transpose(bankbf(6)[0:16, tt * 128:(tt + 1) * 128], cbf[:, tt, :], ident)
    SPE.inc(ins, 1)
    wge(ac, SPE)
    SAC.inc(ac.activation(cfm[0:16, :], bankbf(6)[0:16, :], AF.Copy), 1)

    wge(pe, SAC)
    for tt in range(8):
        bb = 2 + tt // 4
        ins = pe.matmul(PS[:, bb, (tt % 4) * 128:(tt % 4 + 1) * 128],
                        cbf[:, tt, 0:1].broadcast_to([128, 128]), ident,
                        start=True, stop=True)
    SPE.inc(ins, 1)
    wge(ac, SPE)
    SAC.inc(ac.activation(cbuf[0][:, 0:512], bankf(2), AF.Copy), 1)
    SAC.inc(ac.activation(cbuf[0][:, 512:1024], bankf(3), AF.Copy), 1)
    cb_gate = SAC.n

    def w_src(e, kind):
        if e < NE:
            return {'g': rg, 'w1': rw1, 'w2': rw2}[kind][e]
        return {'g': sg, 'w1': sw1, 'w2': sw2}[kind][e - NE]

    def gbias(e, fc):
        return rgbT_sb[:, fc, e:e + 1] if e < NE else sgbT_sb[:, fc, e - NE:e - NE + 1]

    def bbias(e, fc):
        return rb1T_sb[:, fc, e:e + 1] if e < NE else sb1T_sb[:, fc, e - NE:e - NE + 1]

    stg3 = [carve_at(MAP['rstd3'] + i * 4096, 4096, F32) for i in range(3)]
    stg_state = [None, None, None]
    stg_rr = [0]
    cast_rr = [0]

    def stream_chunk(src, dest_flat, reshape3, gate):
        s = stg_rr[0]
        stg_rr[0] = (s + 1) % 3
        if stg_state[s] is not None:
            wge(sy, stg_state[s][0], stg_state[s][1])
        dview = stg3[s].rearrange('p (c f) -> p c f', c=8) if reshape3 else stg3[s]
        SWs[s].inc(sy.dma_start(out=dview, in_=src), 16)
        if cast_rr[0] == 0:
            wge(ac, SWs[s])
            wge(ac, SPE, gate)
            SAC.inc(ac.activation(dest_flat, stg3[s], AF.Copy), 1)
            stg_state[s] = (SAC, SAC.n)
        else:
            wge(ve, SWs[s])
            wge(ve, SPE, gate)
            SDV.inc(ve.tensor_copy(dest_flat, stg3[s]), 1)
            stg_state[s] = (SDV, SDV.n)
        cast_rr[0] ^= 1

    def emit_wdma(e, fc, kind, dest, gate=0):
        if kind in ('g', 'w1'):
            src_full = w_src(e, kind)[:, fc * 128:(fc + 1) * 128]
            for hh in range(2):
                src = src_full[1024 * hh:1024 * (hh + 1), :].rearrange(
                    '(c p) f -> p c f', p=128)
                stream_chunk(src,
                             dest[:, 8 * hh:8 * (hh + 1), :].rearrange('p a b -> p (a b)'),
                             True, gate)
        else:
            src_full = w_src(e, 'w2')[fc * 128:(fc + 1) * 128, :]
            for hh in range(2):
                stream_chunk(src_full[:, 1024 * hh:1024 * (hh + 1)],
                             dest[:, 1024 * hh:1024 * (hh + 1)], False, gate)

    wge(sy, SPO)
    wge(sy, SDV)
    wge(ac, SPO)
    wge(ac, SDV)
    wge(ve, SPO)
    for fc0 in range(8):
        emit_wdma(0, fc0, 'w2', rw2b[:, fc0, :])
    w2_ready = (SAC.n, SDV.n)
    emit_wdma(0, 0, 'g', rgb_[0])
    emit_wdma(0, 0, 'w1', rw1b[0])
    w_ready = [(SAC.n, SDV.n), (0, 0)]

    b01_gate = SAC.n
    b23_gate_dve = SDV.n
    b23_gate_ac = cb_gate
    out_war = [(SDV, 0), (SDV, 0)]
    gsl_gate = SDV.n
    rgb_free = [0, 0]
    rw1_free = [0, 0]
    rw2_free = 0

    for e in range(NEXP):
        routed = e < NE
        for fc in range(8):
            wpar = fc % 2
            if fc + 1 < 8:
                p2 = (fc + 1) % 2
                emit_wdma(e, fc + 1, 'g', rgb_[p2], gate=rgb_free[p2])
                emit_wdma(e, fc + 1, 'w1', rw1b[p2], gate=rw1_free[p2])
                w_ready[p2] = (SAC.n, SDV.n)
            elif e + 1 < NEXP:
                emit_wdma(e + 1, 0, 'g', rgb_[0], gate=rgb_free[0])
                emit_wdma(e + 1, 0, 'w1', rw1b[0], gate=rw1_free[0])
                w_ready[0] = (SAC.n, SDV.n)
            wge(pe, SAC, max(b01_gate, w_ready[wpar][0]))
            wge(pe, SDV, w_ready[wpar][1])
            for ic in range(16):
                pe.matmul(bankf(0), rgb_[wpar][:, ic, :], ln2[:, ic, 0:512],
                          start=(ic == 0), stop=(ic == 15))
                i1 = pe.matmul(bankf(1), rgb_[wpar][:, ic, :], ln2[:, ic, 512:1024],
                               start=(ic == 0), stop=(ic == 15))
            SPE.inc(i1, 1)
            rgb_free[wpar] = SPE.n
            wge(ac, SPE)
            wge(ac, SDV, gsl_gate)
            SAC.inc(ac.activation(gsl[wpar], bankpair(0),
                                  AF.Gelu, bias=gbias(e, fc)), 1)
            b01_gate = SAC.n
            wge(pe, SDV, max(b23_gate_dve, w_ready[wpar][1]))
            wge(pe, SAC, b23_gate_ac)
            for ic in range(16):
                pe.matmul(bankf(2), rw1b[wpar][:, ic, :], ln2[:, ic, 0:512],
                          start=(ic == 0), stop=(ic == 15))
                i1 = pe.matmul(bankf(3), rw1b[wpar][:, ic, :], ln2[:, ic, 512:1024],
                               start=(ic == 0), stop=(ic == 15))
            SPE.inc(i1, 1)
            rw1_free[wpar] = SPE.n
            wge(ve, SPE)
            wge(ve, SAC, b01_gate)
            SDV.inc(ve.scalar_tensor_tensor(hsl[:, fc, :],
                                            bankpair(2), bbias(e, fc),
                                            gsl[wpar],
                                            ALU.add, ALU.mult), 1)
            b23_gate_dve = SDV.n
            gsl_gate = SDV.n
            if routed:
                wge(po, SDV, b23_gate_dve)
                wge(po, SAC, b23_gate_ac)
                SPO.inc(po.tensor_tensor(hsl[:, fc, :], hsl[:, fc, :], cbuf[e % 2], ALU.mult), 1)
        h_dve, h_po = b23_gate_dve, SPO.n
        if e + 1 < NE:
            en = e + 1
            wge(pe, SDV, b23_gate_dve)
            for tt in range(8):
                bb = 2 + tt // 4
                ins = pe.matmul(PS[:, bb, (tt % 4) * 128:(tt % 4 + 1) * 128],
                                cbf[:, tt, en:en + 1].broadcast_to([128, 128]), ident,
                                start=True, stop=True)
            SPE.inc(ins, 1)
            wge(ac, SPE)
            wge(ac, SPO, h_po)
            SAC.inc(ac.activation(cbuf[en % 2][:, 0:512], bankf(2), AF.Copy), 1)
            SAC.inc(ac.activation(cbuf[en % 2][:, 512:1024], bankf(3), AF.Copy), 1)
            b23_gate_ac = SAC.n
        wge(pe, SDV, h_dve)
        wge(pe, SPO, h_po)
        wge(pe, SAC, w2_ready[0])
        wge(pe, SDV, w2_ready[1])
        for dc in range(16):
            pr = dc % 2
            bb0 = 4 if pr == 0 else 6
            ws, wv = out_war[pr]
            wge(pe, ws, wv)
            for fc in range(8):
                pe.matmul(bankf(bb0), rw2b[:, fc, dc * 128:(dc + 1) * 128],
                          hsl[:, fc, 0:512], start=(fc == 0), stop=(fc == 7))
                i1 = pe.matmul(bankf(bb0 + 1), rw2b[:, fc, dc * 128:(dc + 1) * 128],
                               hsl[:, fc, 512:1024], start=(fc == 0), stop=(fc == 7))
            SPE.inc(i1, 1)
            wge(ve, SPE)
            SDV.inc(ve.tensor_tensor(acc[:, dc, :], acc[:, dc, :],
                                     bankpair(bb0), ALU.add), 1)
            out_war[pr] = (SDV, SDV.n)
        rw2_free = SPE.n
        if e + 1 < NEXP:
            for fc0 in range(8):
                emit_wdma(e + 1, fc0, 'w2', rw2b[:, fc0, :], gate=rw2_free)
            w2_ready = (SAC.n, SDV.n)

    for dc in range(16):
        pr = dc % 2
        bb0, bb1 = (4, 5) if pr == 0 else (6, 7)
        ws, wv = out_war[pr]
        wge(pe, ws, wv)
        wge(pe, SAC)
        for th in range(2):
            bb = bb0 if th == 0 else bb1
            pe.matmul(bankf(bb), rb2b[0:16, dc * 128:(dc + 1) * 128],
                      cfm[0:16, th * 512:(th + 1) * 512], start=True, stop=False)
            ins = pe.matmul(bankf(bb), sb2b[0:2, dc * 128:(dc + 1) * 128],
                            ones_bf[0:2, th * 512:(th + 1) * 512], start=False, stop=True)
        SPE.inc(ins, 1)
        wge(ve, SPE)
        SDV.inc(ve.tensor_tensor(acc[:, dc, :], acc[:, dc, :],
                                 PS[:, bb0:bb0 + 2, :].rearrange('p a b -> p (a b)'),
                                 ALU.add), 1)
        out_war[pr] = (SDV, SDV.n)

    wge(sy, SDV)
    SOUT.inc(sy.dma_start(out=o.rearrange('(c p) t -> p c t', p=128), in_=acc), 16)
    wge(sy, SOUT)
    return nc


def kernel(**inputs):
    f32 = np.float32
    bf = ml_dtypes.bfloat16
    x = np.asarray(inputs['L0_attn_x'], f32)
    Wq = np.asarray(inputs['L0_attn_Wq_w'], f32)
    Wdkv = np.asarray(inputs['L0_attn_Wdkv_w'], f32)
    Wuk = np.asarray(inputs['L0_attn_Wuk_w'], f32)
    Wuv = np.asarray(inputs['L0_attn_Wuv_w'], f32)
    Wo = np.asarray(inputs['L0_attn_Wo_w'], f32)

    def biasT(v, nch):
        E = v.shape[0]
        return np.ascontiguousarray(v.T.reshape(nch, 128, E).transpose(1, 0, 2))

    shared = dict(
        wq=np.ascontiguousarray(Wq),
        wqt=np.ascontiguousarray(Wq.T),
        wuk=np.ascontiguousarray(Wuk),
        wuv=np.ascontiguousarray(Wuv),
        wot=np.ascontiguousarray(Wo.T),
        wdkvt=np.ascontiguousarray(Wdkv.T),
        rg=np.ascontiguousarray(np.asarray(inputs['L0_ffn_rg'], f32)),
        rw1=np.ascontiguousarray(np.asarray(inputs['L0_ffn_rw1'], f32)),
        rw2=np.ascontiguousarray(np.asarray(inputs['L0_ffn_rw2'], f32)),
        sg=np.ascontiguousarray(np.asarray(inputs['L0_ffn_sg'], f32)),
        sw1=np.ascontiguousarray(np.asarray(inputs['L0_ffn_sw1'], f32)),
        sw2=np.ascontiguousarray(np.asarray(inputs['L0_ffn_sw2'], f32)),
        wa=np.ascontiguousarray(np.asarray(inputs['L0_ffn_wa'], f32)),
        rgbT=biasT(np.asarray(inputs['L0_ffn_rgb'], f32), 8),
        rb1T=biasT(np.asarray(inputs['L0_ffn_rb1'], f32), 8),
        sgbT=biasT(np.asarray(inputs['L0_ffn_sgb'], f32), 8),
        sb1T=biasT(np.asarray(inputs['L0_ffn_sb1'], f32), 8),
        rb2=np.ascontiguousarray(np.asarray(inputs['L0_ffn_rb2'], f32)),
        sb2=np.ascontiguousarray(np.asarray(inputs['L0_ffn_sb2'], f32)),
        identD=np.eye(128, dtype=f32).astype(bf),
    )

    xT = [np.ascontiguousarray(x[b].T) for b in range(B)]

    in_maps, tiles = [], []
    for k in range(8):
        b, r = k // 4, k % 4
        Ts = [4 * (c - 1) + ((r + c - 1) % 4) for c in range(1, NSLOT + 1)]
        tiles.append((b, Ts))
        xo = np.ascontiguousarray(
            np.concatenate([xT[b][:, 128 * t:128 * t + 128] for t in Ts], axis=1))
        kk = np.arange(512)[None, :]
        qq = np.arange(128)[:, None]
        mk = np.zeros((128, NSLOT, 512), dtype=f32)
        for s in range(NSLOT):
            m = (r + s) % 4
            mk[:, s, :] = np.where(kk <= 128 * m + qq, 0.0, NEG)
        im = dict(shared)
        im['xall'] = xT[b]
        im['xown'] = xo
        im['masksD'] = mk.astype(bf)
        in_maps.append(im)

    nc = build_program()
    res = run_bass_kernel_spmd(nc, in_maps, core_ids=list(range(8)))

    out = np.empty((B, T, D), dtype=f32)
    for k in range(8):
        b, Ts = tiles[k]
        ok = res.results[k]['o']
        for s, t in enumerate(Ts):
            out[b, 128 * t:128 * t + 128, :] = ok[:, 128 * s:128 * (s + 1)].T
    return out


# version 10 (final)
# speedup vs baseline: 1.1967x; 1.0316x over previous
import numpy as np
import ml_dtypes
import concourse.bass as bass
import concourse.mybir as mybir
from concourse.bass_utils import run_bass_kernel_spmd

F32 = mybir.dt.float32
BF16 = mybir.dt.bfloat16
U8 = mybir.dt.uint8
AF = mybir.ActivationFunctionType
ALU = mybir.AluOpType
AX = mybir.AxisListType

NH, HS, NL, D, T, B = 16, 128, 512, 2048, 4096, 2
NTOK = 1024
NSLOT = 8
NE, NS, FF = 16, 2, 1024
NEXP = NE + NS
ISCALE = float(1.0 / np.sqrt(128.0))
NEG = -1.0e30
ARENA = 211968

MAP = {}
MAP['ident'] = 0
MAP['ones_bf'] = 256
MAP['ones_f'] = 2304
MAP['rows'] = 2816
MAP['amisc'] = 11008
MAP['x1'] = 12032
A0 = 44800
MAP['wbig'] = A0
MAP['wsm'] = A0 + 65536
MAP['wst0'] = A0 + 81920
MAP['keffo'] = A0 + 114688
MAP['vmato'] = A0 + 131072
MAP['wqt'] = A0
MAP['xst'] = A0 + 65536
MAP['lng'] = A0 + 98304
MAP['qbuf'] = A0 + 114688
MAP['tmp1a'] = A0 + 122880
MAP['rstd1a'] = A0 + 139264
MAP['keff_sb'] = A0 + 143360
MAP['qlbuf'] = A0 + 159744
MAP['wdkvt'] = A0
MAP['xst_b'] = A0 + 16384
MAP['lng_b'] = A0 + 49152
MAP['tmp1b'] = A0 + 65536
MAP['rstd1b'] = A0 + 81920
MAP['ckv_tm'] = 136192
MAP['ckv_fm'] = 168960
MAP['pbuf'] = A0
MAP['pt'] = A0 + 16384
MAP['qlath'] = A0 + 24576
MAP['vmath'] = A0 + 40960
MAP['masks'] = A0 + 43008
MAP['ylat_all'] = A0 + 51200
MAP['ylatt_all'] = A0 + 59392
MAP['pbuf3'] = A0 + 67584
MAP['rw2'] = 12032
MAP['ln2'] = A0
MAP['hsl'] = A0 + 32768
MAP['rgb_'] = A0 + 49152
MAP['rw1b'] = A0 + 57344
MAP['tmp3'] = A0 + 32768
MAP['gsl'] = A0 + 65536
MAP['cfm'] = A0 + 69632
MAP['cbf'] = A0 + 71680
MAP['cb'] = A0 + 72192
MAP['aff'] = A0 + 76288
MAP['wab'] = A0 + 76800
MAP['rgbT'] = A0 + 77312
MAP['rb1T'] = A0 + 77824
MAP['sgbT'] = A0 + 78336
MAP['sb1T'] = A0 + 78848
MAP['rb2b'] = A0 + 79360
MAP['sb2b'] = A0 + 83456
MAP['sqs'] = A0 + 87552
MAP['tks'] = A0 + 88576
MAP['rstd3'] = A0 + 89088
MAP['stg3'] = A0 + 93184
MAP['acc'] = A0 + 101376
assert MAP['acc'] + 65536 <= ARENA


def build_program():
    nc = bass.Bass("TRN2", target_bir_lowering=False)

    ap_ = lambda h: h.ap()
    xall = ap_(nc.dram_tensor('xall', [D, T], F32, kind='ExternalInput'))
    xown = ap_(nc.dram_tensor('xown', [D, NTOK], F32, kind='ExternalInput'))
    wq = ap_(nc.dram_tensor('wq', [D, D], F32, kind='ExternalInput'))
    wqt = ap_(nc.dram_tensor('wqt', [D, D], F32, kind='ExternalInput'))
    wuk = ap_(nc.dram_tensor('wuk', [D, NL], F32, kind='ExternalInput'))
    wuv = ap_(nc.dram_tensor('wuv', [D, NL], F32, kind='ExternalInput'))
    wot = ap_(nc.dram_tensor('wot', [D, D], F32, kind='ExternalInput'))
    wdkvt = ap_(nc.dram_tensor('wdkvt', [D, NL], F32, kind='ExternalInput'))
    rg = ap_(nc.dram_tensor('rg', [NE, D, FF], F32, kind='ExternalInput'))
    rw1 = ap_(nc.dram_tensor('rw1', [NE, D, FF], F32, kind='ExternalInput'))
    rw2 = ap_(nc.dram_tensor('rw2', [NE, FF, D], F32, kind='ExternalInput'))
    sg = ap_(nc.dram_tensor('sg', [NS, D, FF], F32, kind='ExternalInput'))
    sw1 = ap_(nc.dram_tensor('sw1', [NS, D, FF], F32, kind='ExternalInput'))
    sw2 = ap_(nc.dram_tensor('sw2', [NS, FF, D], F32, kind='ExternalInput'))
    wa = ap_(nc.dram_tensor('wa', [D, NE], F32, kind='ExternalInput'))
    rgbT = ap_(nc.dram_tensor('rgbT', [128, 8, NE], F32, kind='ExternalInput'))
    rb1T = ap_(nc.dram_tensor('rb1T', [128, 8, NE], F32, kind='ExternalInput'))
    sgbT = ap_(nc.dram_tensor('sgbT', [128, 8, NS], F32, kind='ExternalInput'))
    sb1T = ap_(nc.dram_tensor('sb1T', [128, 8, NS], F32, kind='ExternalInput'))
    rb2 = ap_(nc.dram_tensor('rb2', [NE, D], F32, kind='ExternalInput'))
    sb2 = ap_(nc.dram_tensor('sb2', [NS, D], F32, kind='ExternalInput'))
    identD = ap_(nc.dram_tensor('identD', [128, 128], BF16, kind='ExternalInput'))
    masksD = ap_(nc.dram_tensor('masksD', [128, 8, 512], BF16, kind='ExternalInput'))
    o = ap_(nc.dram_tensor('o', [D, NTOK], F32, kind='ExternalOutput'))
    qlscr = ap_(nc.dram_tensor('qlscr', [NH, 128, 4, NTOK], BF16, kind='Internal'))
    kescr = ap_(nc.dram_tensor('kescr', [128, 16, 512], BF16, kind='Internal'))
    vmscr = ap_(nc.dram_tensor('vmscr', [128, 4, 2048], BF16, kind='Internal'))

    arena = nc.alloc_sbuf_tensor('arena', [128, ARENA], U8)
    AR = arena.ap()

    def carve(name, shape, dtype):
        off = MAP[name]
        es = mybir.dt.size(dtype)
        n = int(np.prod(shape[1:])) * es
        v = AR[:, off:off + n].bitcast(dtype)
        if len(shape) == 3:
            v = v.rearrange('p (a b) -> p a b', a=shape[1])
        return v

    def carve_at(off, nbytes, dtype):
        return AR[:, off:off + nbytes].bitcast(dtype)

    pball = nc.alloc_psum_tensor('pball', [128, 8, 512], F32)
    PS = pball.ap()
    bankbf = lambda i: PS[:, i, :].bitcast(BF16)

    def bankf(i, lo=0, hi=512):
        return PS[:, i, lo:hi]

    class Sem:
        __slots__ = ('h', 'n')
        def __init__(self, name):
            self.h = nc.alloc_semaphore(name)
            self.n = 0
        def inc(self, instr, k=1):
            instr.then_inc(self.h, k)
            self.n += k
            return instr

    SW = Sem('sw'); SQO = Sem('sqo')
    SXp = [Sem('sx0'), Sem('sx1')]
    SQLp = [Sem('sql0'), Sem('sql1')]
    SWp = [Sem('sw0'), Sem('sw1')]
    SWs = [Sem('sws0'), Sem('sws1'), Sem('sws2')]
    SQIp = [Sem('sqi0'), Sem('sqi1')]
    SSM = Sem('ssm'); SOUT = Sem('sout')
    SPE = Sem('spe'); SAC = Sem('sac'); SDV = Sem('sdv'); SPO = Sem('spo')

    pe, ve, ac, po, sy = nc.tensor, nc.vector, nc.scalar, nc.gpsimd, nc.sync

    lastw = {}

    def wge(eng, s, val=None):
        v = s.n if val is None else val
        if v <= 0:
            return
        k = (id(eng), id(s))
        if lastw.get(k, -1) >= v:
            return
        eng.wait_ge(s.h, v)
        lastw[k] = v

    def dve_edge():
        wge(ve, SDV)

    def ac_edge():
        wge(ac, SAC)

    for s in (SW, SXp[0], SXp[1], SQLp[0], SQLp[1], SWp[0], SWp[1], SWs[0], SWs[1], SWs[2],
              SQIp[0], SQIp[1], SQO, SSM, SOUT, SPE, SAC, SDV, SPO):
        po.sem_clear(s.h)
    nc.all_engine_barrier()

    ident = carve('ident', [128, 128], BF16)
    ones_bf = carve('ones_bf', [128, 1024], BF16)
    ones_f = carve('ones_f', [128, 128], F32)
    rows = carve('rows', [128, 2048], F32)
    amisc = carve('amisc', [128, 256], F32)
    sqpp = [carve_at(MAP['amisc'], 512, BF16), carve_at(MAP['amisc'] + 512, 512, BF16)]
    x1 = carve('x1', [128, 16, 1024], BF16)

    SSM.inc(sy.dma_start(out=ident, in_=identD), 16)
    SSM.inc(po.memset(ones_bf, 1.0), 1)
    SSM.inc(po.memset(ones_f, 1.0), 1)

    lw_last = {}

    def load_weight_bf(src, dest_bf, nrow_chunks, cols, stage_off, stage_cap):
        per = max(1, stage_cap // cols)
        stg = [carve_at(stage_off + i * stage_cap * 4, stage_cap * 4, F32) for i in (0, 1)]
        done, i, pend = 0, 0, []
        src3 = src.rearrange('(c p) f -> p c f', p=128)
        while done < nrow_chunks or pend:
            while done < nrow_chunks and len(pend) < 2:
                m = min(per, nrow_chunks - done)
                par = i % 2
                dst = stg[par][:, 0:m * cols].rearrange('p (c f) -> p c f', c=m)
                last = lw_last.get((stage_off, par))
                if last is not None:
                    wge(sy, last[0], last[1])
                SWp[par].inc(sy.dma_start(out=dst, in_=src3[:, done:done + m, :]), 16)
                pend.append((par, done, m, SWp[par].n))
                done += m
                i += 1
            sidx, st_c, m, swn = pend.pop(0)
            sv = stg[sidx][:, 0:m * cols]
            dv = dest_bf[:, st_c:st_c + m, :].rearrange('p a b -> p (a b)')
            if sidx == 0:
                wge(ac, SWp[sidx], swn)
                SAC.inc(ac.activation(dv, sv, AF.Copy), 1)
                lw_last[(stage_off, sidx)] = (SAC, SAC.n)
            else:
                wge(ve, SWp[sidx], swn)
                SDV.inc(ve.tensor_copy(dv, sv), 1)
                lw_last[(stage_off, sidx)] = (SDV, SDV.n)

    lnsq_gate = [0, 0]
    ln_rows_free = [0]
    ln_apply_po = [0, 0]
    ln_apply_ve = [0, 0]
    ln_tmp_free = [0]
    ln_bcast = [0, 0]

    def ln_S(x_f32, ntok, rstd_sb, mrs_sb, par):
        wge(pe, SXp[par])
        wge(ac, SXp[par])
        wge(pe, SAC)
        for c in range(16):
            ins = pe.matmul(PS[0:1, 0, 0:ntok], ones_f[:, 0:1], x_f32[:, c, :],
                            start=(c == 0), stop=(c == 15))
        SPE.inc(ins, 1)
        for c in range(16):
            sq = sqpp[c % 2]
            wge(ac, SPE, lnsq_gate[c % 2])
            SAC.inc(ac.activation(sq[:, 0:ntok], x_f32[:, c, :], AF.Square), 1)
            wge(pe, SAC)
            ins = pe.matmul(PS[0:1, 1, 0:ntok], ones_bf[:, 0:1], sq[:, 0:ntok],
                            start=(c == 0), stop=(c == 15))
            SPE.inc(ins, 1)
            lnsq_gate[c % 2] = SPE.n
        mean = rows[0:1, 0:ntok]
        msq = rows[0:1, 512:512 + ntok]
        mean2 = rows[0:1, 1024:1024 + ntok]
        rstd = rows[0:1, 1536:1536 + ntok]
        wge(ac, SPE)
        wge(ac, SDV, ln_rows_free[0])
        SAC.inc(ac.activation(mean, PS[0:1, 0, 0:ntok], AF.Copy, scale=1.0 / D), 1)
        SAC.inc(ac.activation(msq, PS[0:1, 1, 0:ntok], AF.Copy, scale=1.0 / D), 1)
        ac_edge()
        SAC.inc(ac.activation(mean2, mean, AF.Square), 1)
        wge(ve, SAC)
        SDV.inc(ve.scalar_tensor_tensor(msq, msq, 1.0e-5, mean2,
                                        ALU.add, ALU.subtract), 1)
        wge(ac, SDV)
        SAC.inc(ac.activation(mean2, msq, AF.Ln), 1)
        ac_edge()
        SAC.inc(ac.activation(rstd, mean2, AF.Exp, scale=-0.5), 1)
        ac_edge()
        wge(ve, SAC)
        SDV.inc(ve.tensor_mul(mean2, mean, rstd), 1)
        ln_rows_free[0] = SDV.n
        wge(pe, SAC)
        wge(pe, SDV)
        SPE.inc(pe.matmul(PS[:, 6, 0:ntok], ones_f[0:1, :], rstd, start=True, stop=True), 1)
        SPE.inc(pe.matmul(PS[:, 7, 0:ntok], ones_f[0:1, :], mean2, start=True, stop=True), 1)
        wge(ac, SPE)
        wge(ac, SPO, ln_apply_po[par])
        wge(ac, SDV, ln_apply_ve[par])
        SAC.inc(ac.activation(rstd_sb[:, 0:ntok], PS[:, 6, 0:ntok], AF.Copy), 1)
        SAC.inc(ac.activation(mrs_sb[:, 0:ntok], PS[:, 7, 0:ntok], AF.Copy), 1)
        ln_bcast[par] = SAC.n

    def ln_T(x_f32, ln_out, ntok, rstd_sb, mrs_sb, tmp_big, par, lng_free_val):
        wge(po, SAC, ln_bcast[par])
        wge(po, SDV, ln_tmp_free[0])
        for c in range(16):
            ins = po.tensor_mul(tmp_big[:, c, 0:ntok], x_f32[:, c, :], rstd_sb[:, 0:ntok])
        SPO.inc(ins, 1)
        ln_apply_po[par] = SPO.n
        wge(ve, SPO)
        wge(ve, SPE, lng_free_val)
        for c in range(16):
            ins = ve.tensor_sub(ln_out[:, c, :], tmp_big[:, c, 0:ntok], mrs_sb[:, 0:ntok])
        SDV.inc(ins, 1)
        ln_apply_ve[par] = SDV.n
        ln_tmp_free[0] = SDV.n

    wbig = carve('wbig', [128, 16, 2048], BF16)
    wsm = carve('wsm', [128, 16, 512], BF16)
    keffo = carve('keffo', [128, 16, 512], BF16)
    vmato = carve('vmato', [128, 4, 2048], BF16)

    load_weight_bf(wq, wbig, 16, 2048, MAP['wst0'], 4096)
    load_weight_bf(wuk, wsm, 16, 512, MAP['wst0'], 4096)
    wge(pe, SAC)
    wge(pe, SDV)
    for pc in range(16):
        b = pc % 2
        wge(pe, SAC)
        for jc in range(16):
            ins = pe.matmul(bankf(b), wbig[:, jc, pc * 128:(pc + 1) * 128], wsm[:, jc, :],
                            start=(jc == 0), stop=(jc == 15))
        SPE.inc(ins, 1)
        wge(ac, SPE)
        SAC.inc(ac.activation(keffo[:, pc, :], bankf(b), AF.Copy), 1)
    wge(sy, SAC)
    SQO.inc(sy.dma_start(out=kescr, in_=keffo), 16)

    load_weight_bf(wot, wbig, 16, 2048, MAP['wst0'], 4096)
    load_weight_bf(wuv, wsm, 16, 512, MAP['wst0'], 4096)
    wge(pe, SAC)
    wge(pe, SDV)
    for lc in range(4):
        for ds in range(4):
            b = 2 + (lc * 4 + ds) % 2
            wge(pe, SAC)
            for mc in range(16):
                ins = pe.matmul(bankf(b), wsm[:, mc, lc * 128:(lc + 1) * 128],
                                wbig[:, mc, ds * 512:(ds + 1) * 512],
                                start=(mc == 0), stop=(mc == 15))
            SPE.inc(ins, 1)
            wge(ac, SPE)
            SAC.inc(ac.activation(vmato[:, lc, ds * 512:(ds + 1) * 512], bankf(b), AF.Copy), 1)
    wge(sy, SAC)
    SQO.inc(sy.dma_start(out=vmscr, in_=vmato), 16)

    nc.all_engine_barrier()

    wqt_sb = carve('wqt', [128, 16, 2048], BF16)
    load_weight_bf(wqt, wqt_sb, 16, 2048, MAP['wst0'], 4096)
    nc.all_engine_barrier()
    keff_sb = carve('keff_sb', [128, 16, 512], BF16)
    qlbuf = [carve_at(MAP['qlbuf'] + i * 2048, 2048, BF16) for i in (0, 1)]
    wge(sy, SQO)
    SSM.inc(sy.dma_start(out=keff_sb, in_=kescr), 16)
    wge(po, SQO)
    wge(ac, SQO)

    xst = [carve_at(MAP['xst'] + i * 16384, 16384, F32).rearrange('p (a b) -> p a b', a=16)
           for i in (0, 1)]
    lng = [carve_at(MAP['lng'] + i * 8192, 8192, BF16).rearrange('p (a b) -> p a b', a=16)
           for i in (0, 1)]
    qbuf = carve('qbuf', [128, 16, 256], BF16)
    tmp1a = carve('tmp1a', [128, 16, 256], F32)
    rstd1a = [carve_at(MAP['rstd1a'] + i * 2048, 1024, F32) for i in (0, 1)]
    mrs1a = [carve_at(MAP['rstd1a'] + i * 2048 + 1024, 1024, F32) for i in (0, 1)]

    cons_free = {2: 0, 3: 0, 4: 0, 5: 0}
    lng_free = [0, 0]
    sx_done = {}
    qb_free = [0]
    ql_free = [0]

    def stage1a_T(g):
        par = g % 2
        wge(po, SXp[g % 2], sx_done[g])
        SPO.inc(po.tensor_copy(x1[:, :, g * 256:(g + 1) * 256], xst[par]), 1)
        ln_T(xst[par], lng[par], 256, rstd1a[par], mrs1a[par], tmp1a, par,
             lng_free[par])
        wge(pe, SDV, ln_apply_ve[par])
        for jc in range(16):
            b = 2 + jc % 2
            wge(pe, SAC, cons_free[b])
            for ic in range(16):
                ins = pe.matmul(bankf(b, 0, 256), wqt_sb[:, ic, jc * 128:(jc + 1) * 128],
                                lng[par][:, ic, :], start=(ic == 0), stop=(ic == 15))
            SPE.inc(ins, 1)
            wge(ac, SPE)
            wge(ac, SPE, qb_free[0])
            SAC.inc(ac.activation(qbuf[:, jc, :], bankf(b, 0, 256), AF.Copy), 1)
            cons_free[b] = SAC.n
        lng_free[par] = SPE.n
        qv = SAC.n
        wge(pe, SSM)
        wge(pe, SAC, qv)
        for hh in range(16):
            wge(pe, SDV, ql_free[0])
            for lc in range(4):
                ins = pe.matmul(PS[:, 4 + lc // 2, (lc % 2) * 256:(lc % 2) * 256 + 256],
                                keff_sb[:, hh, lc * 128:(lc + 1) * 128],
                                qbuf[:, hh, :], start=True, stop=True)
            SPE.inc(ins, 1)
            wge(ve, SPE)
            wge(ve, SQLp[hh % 2])
            SDV.inc(ve.tensor_scalar(qlbuf[hh % 2][:, 0:512],
                                     PS[:, 4, :], ISCALE, None, ALU.mult), 1)
            SDV.inc(ve.tensor_scalar(qlbuf[hh % 2][:, 512:1024],
                                     PS[:, 5, :], ISCALE, None, ALU.mult), 1)
            ql_free[0] = SDV.n
            wge(sy, SDV, ql_free[0])
            SQLp[hh % 2].inc(
                sy.dma_start(out=qlscr[hh][:, :, g * 256:(g + 1) * 256],
                             in_=qlbuf[hh % 2].rearrange('p (a b) -> p a b', a=4)), 16)
        qb_free[0] = SPE.n

    for g in range(4):
        wge(sy, SPO)
        SXp[g % 2].inc(sy.dma_start(out=xst[g % 2],
                            in_=xown[:, g * 256:(g + 1) * 256].rearrange('(c p) t -> p c t', p=128)), 16)
        sx_done[g] = SXp[g % 2].n
        ln_S(xst[g % 2], 256, rstd1a[g % 2], mrs1a[g % 2], g % 2)
        if g >= 1:
            stage1a_T(g - 1)
    stage1a_T(3)

    nc.all_engine_barrier()

    wdkvt_sb = carve('wdkvt', [128, 16, 512], BF16)
    load_weight_bf(wdkvt, wdkvt_sb, 16, 512, MAP['xst_b'], 4096)
    nc.all_engine_barrier()
    wge(ac, SQO)

    xstb = [carve_at(MAP['xst_b'] + i * 16384, 16384, F32).rearrange('p (a b) -> p a b', a=16)
            for i in (0, 1)]
    lngb = [carve_at(MAP['lng_b'] + i * 8192, 8192, BF16).rearrange('p (a b) -> p a b', a=16)
            for i in (0, 1)]
    tmp1b = carve('tmp1b', [128, 16, 256], F32)
    rstd1b = [carve_at(MAP['rstd1b'] + i * 2048, 1024, F32) for i in (0, 1)]
    mrs1b = [carve_at(MAP['rstd1b'] + i * 2048 + 1024, 1024, F32) for i in (0, 1)]
    ckv_tm = carve('ckv_tm', [128, 32, 512], BF16)
    ckv_fm = carve('ckv_fm', [128, 4, 4096], BF16)

    def stage1b_T(g):
        par = g % 2
        ln_T(xstb[par], lngb[par], 256, rstd1b[par], mrs1b[par], tmp1b, par,
             lng_free[par])
        wge(pe, SDV, ln_apply_ve[par])
        for kt in range(2):
            b = 2 + kt
            wge(pe, SAC, cons_free[b])
            for ic in range(16):
                ins = pe.matmul(bankf(b), lngb[par][:, ic, kt * 128:(kt + 1) * 128],
                                wdkvt_sb[:, ic, :], start=(ic == 0), stop=(ic == 15))
            SPE.inc(ins, 1)
            wge(ac, SPE)
            SAC.inc(ac.activation(ckv_tm[:, g * 2 + kt, :], bankf(b), AF.Copy), 1)
            cons_free[b] = SAC.n
        for lc in range(4):
            b = 4 + lc % 2
            wge(pe, SAC, cons_free[b])
            for ic in range(16):
                ins = pe.matmul(bankf(b, 0, 256), wdkvt_sb[:, ic, lc * 128:(lc + 1) * 128],
                                lngb[par][:, ic, :], start=(ic == 0), stop=(ic == 15))
            SPE.inc(ins, 1)
            wge(ac, SPE)
            SAC.inc(ac.activation(ckv_fm[:, lc, g * 256:(g + 1) * 256], bankf(b, 0, 256), AF.Copy), 1)
            cons_free[b] = SAC.n
        lng_free[par] = SPE.n

    for g in range(16):
        wge(sy, SPO)
        SXp[g % 2].inc(sy.dma_start(out=xstb[g % 2],
                            in_=xall[:, g * 256:(g + 1) * 256].rearrange('(c p) t -> p c t', p=128)), 16)
        ln_S(xstb[g % 2], 256, rstd1b[g % 2], mrs1b[g % 2], g % 2)
        if g >= 1:
            stage1b_T(g - 1)
    stage1b_T(15)

    nc.all_engine_barrier()

    pbuf = [carve_at(MAP['pbuf'] + i * 8192, 8192, BF16) for i in (0, 1)] + \
           [carve_at(MAP['pbuf3'], 8192, BF16)]
    ptb = carve('pt', [128, 32, 128], BF16)
    qlath = [carve_at(MAP['qlath'] + i * 8192, 8192, BF16).rearrange('p (a b) -> p a b', a=4)
             for i in (0, 1)]
    vmath = [carve_at(MAP['vmath'] + i * 1024, 1024, BF16).rearrange('p (a b) -> p a b', a=4)
             for i in (0, 1)]
    ylat_all = carve('ylat_all', [128, 8, 512], BF16)
    ylatt_all = carve('ylatt_all', [128, 4, 1024], BF16)
    masks = carve('masks', [128, 8, 512], BF16)
    nmx = [amisc[:, 80 * p:80 * p + 16] for p in range(3)]
    sums = [amisc[:, 80 * p + 16:80 * p + 32] for p in range(3)]
    rr = [amisc[:, 80 * p + 32:80 * p + 48] for p in range(3)]
    spr = [amisc[:, 80 * p + 48:80 * p + 64] for p in range(3)]
    negm = [amisc[:, 80 * p + 64:80 * p + 65] for p in range(3)]
    stot = [amisc[:, 80 * p + 65:80 * p + 66] for p in range(3)]
    recip = [amisc[:, 80 * p + 66:80 * p + 67] for p in range(3)]

    wge(sy, SQO)
    wge(sy, SQLp[0])
    wge(sy, SQLp[1])
    SSM.inc(sy.dma_start(out=masks, in_=masksD), 16)

    qh_free = [0, 0]

    def load_head(h):
        par = h % 2
        wge(sy, SPE, qh_free[par])
        SQIp[par].inc(sy.dma_start(out=qlath[par], in_=qlscr[h]), 16)
        SQIp[par].inc(sy.dma_start(out=vmath[par], in_=vmscr[:, :, h * 128:(h + 1) * 128]), 16)

    load_head(0)
    wge(pe, SSM)
    snap_p_free = [0, 0, 0]
    snap_sb = [0, 0, 0]
    nm_free = [0, 0, 0]
    tb_free = {3: (SDV, 0), 7: (SDV, 0)}
    b45_free = {4: 0, 5: 0}


    def attn_a(it, h, si):
        c = si + 1
        hp = h % 2
        par = it % 3
        if si == 0:
            wge(pe, SQIp[hp], 32 * (h // 2 + 1))
        if si == 2 and h + 1 < NH:
            load_head(h + 1)
        wge(ve, SAC, nm_free[par])
        for kc in range(c):
            b = kc % 3
            wge(pe, SAC, snap_sb[b])
            for lc in range(4):
                ins = pe.matmul(bankf(b), qlath[hp][:, lc, si * 128:(si + 1) * 128],
                                ckv_fm[:, lc, kc * 512:(kc + 1) * 512],
                                start=(lc == 0), stop=(lc == 3))
            SPE.inc(ins, 1)
            wge(ve, SPE)
            if kc == c - 1:
                SDV.inc(ve.tensor_tensor(bankf(b), bankf(b), masks[:, si, :], ALU.add), 1)
                dve_edge()
            SDV.inc(ve.reduce_max(nmx[par][:, kc:kc + 1], bankf(b), axis=AX.X, negate=True), 1)
            wge(ac, SDV)
            wge(ac, SPE, snap_p_free[par])
            SAC.inc(ac.activation(pbuf[par][:, kc * 512:(kc + 1) * 512], bankf(b), AF.Exp,
                                  bias=nmx[par][:, kc:kc + 1],
                                  accum_out=sums[par][:, kc:kc + 1]), 1)
            snap_sb[b] = SAC.n
        wge(ve, SAC)
        SDV.inc(ve.tensor_reduce(negm[par], nmx[par][:, 0:c], AX.X, ALU.min), 1)
        dve_edge()
        wge(ac, SDV)
        SAC.inc(ac.activation(rr[par][:, 0:c], nmx[par][:, 0:c], AF.Exp,
                              bias=negm[par], scale=-1.0), 1)
        nm_free[par] = SAC.n
        wge(ve, SAC)
        SDV.inc(ve.tensor_mul(spr[par][:, 0:c], sums[par][:, 0:c], rr[par][:, 0:c]), 1)
        dve_edge()
        SDV.inc(ve.reduce_sum(stot[par], spr[par][:, 0:c], axis=AX.X), 1)
        dve_edge()
        SDV.inc(ve.reciprocal(recip[par], stot[par]), 1)
        for kc in range(c):
            SDV.inc(ve.tensor_scalar(pbuf[par][:, kc * 512:(kc + 1) * 512],
                                     pbuf[par][:, kc * 512:(kc + 1) * 512],
                                     rr[par][:, kc:kc + 1], None, ALU.mult), 1)
        return dict(c=c, hp=hp, par=par, h=h, si=si, it=it, p_ready=SDV.n)

    def attn_b(st):
        c, hp, par, h, si = st['c'], st['hp'], st['par'], st['h'], st['si']
        wge(pe, SDV, st['p_ready'])
        for j4 in range(c):
            bb = 3 if j4 % 2 == 0 else 7
            wge(pe, tb_free[bb][0], tb_free[bb][1])
            for jj in range(4):
                ins = pe.transpose(bankbf(bb)[:, jj * 128:(jj + 1) * 128],
                                   pbuf[par][:, (j4 * 4 + jj) * 128:(j4 * 4 + jj + 1) * 128],
                                   ident)
            SPE.inc(ins, 1)
            wge(ac, SPE)
            SAC.inc(ac.activation(
                ptb[:, j4 * 4:(j4 + 1) * 4, :].rearrange('p a b -> p (a b)'),
                bankbf(bb)[:, 0:512], AF.Copy), 1)
            tb_free[bb] = (SAC, SAC.n)
        snap_p_free[par] = SPE.n
        pvb = 4 + st['it'] % 2
        wge(pe, tb_free[3][0], tb_free[3][1])
        wge(pe, tb_free[7][0], tb_free[7][1])
        wge(pe, SDV, b45_free[pvb])
        for j in range(4 * c):
            ins = pe.matmul(bankf(pvb), ptb[:, j, :], ckv_tm[:, j, :],
                            start=(j == 0), stop=(j == 4 * c - 1))
        SPE.inc(ins, 1)
        wge(ve, SPE)
        SDV.inc(ve.tensor_scalar(ylat_all[:, si, :], bankf(pvb),
                                 recip[par][:, 0:1], None, ALU.mult), 1)
        b45_free[pvb] = SDV.n

    def attn_epi(h):
        hp = h % 2
        ya = max(b45_free[4], b45_free[5])
        for lc in range(4):
            bb = 3 if lc % 2 == 0 else 7
            wge(pe, SDV, ya)
            wge(pe, tb_free[bb][0], tb_free[bb][1])
            for si in range(8):
                ins = pe.transpose(bankbf(bb)[:, si * 128:(si + 1) * 128],
                                   ylat_all[:, si, lc * 128:(lc + 1) * 128], ident)
            SPE.inc(ins, 1)
            wge(ac, SPE)
            SAC.inc(ac.activation(ylatt_all[:, lc, :], bankbf(bb)[:, 0:1024],
                                  AF.Copy), 1)
            tb_free[bb] = (SAC, SAC.n)
        ylt = SAC.n
        for th in range(2):
            bb = 4 + th
            wge(pe, SAC, ylt)
            wge(pe, SDV, b45_free[bb])
            for lc in range(4):
                ins = pe.matmul(PS[:, bb, :], vmath[hp][:, lc, :],
                                ylatt_all[:, lc, th * 512:(th + 1) * 512],
                                start=(lc == 0), stop=(lc == 3))
            SPE.inc(ins, 1)
            wge(ve, SPE)
            SDV.inc(ve.tensor_tensor(x1[:, h, th * 512:(th + 1) * 512],
                                     x1[:, h, th * 512:(th + 1) * 512],
                                     PS[:, bb, :], ALU.add), 1)
            b45_free[bb] = SDV.n
        qh_free[hp] = SPE.n

    slots = [(h, si) for h in range(NH) for si in range(NSLOT)]
    pend2 = []
    for it, (h, si) in enumerate(slots):
        if len(pend2) >= 3:
            stb = pend2.pop(0)
            attn_b(stb)
            if stb['si'] == NSLOT - 1:
                attn_epi(stb['h'])
        pend2.append(attn_a(it, h, si))
    for stb in pend2:
        attn_b(stb)
        if stb['si'] == NSLOT - 1:
            attn_epi(stb['h'])

    nc.all_engine_barrier()

    ln2 = carve('ln2', [128, 16, 1024], BF16)
    hsl = carve('hsl', [128, 8, 1024], BF16)
    rgb_ = [carve_at(MAP['rgb_'] + i * 4096, 4096, BF16).rearrange('p (a b) -> p a b', a=16)
            for i in (0, 1)]
    rw1b = [carve_at(MAP['rw1b'] + i * 4096, 4096, BF16).rearrange('p (a b) -> p a b', a=16)
            for i in (0, 1)]
    rw2b = carve('rw2', [128, 8, 2048], BF16)
    gsl = [carve_at(MAP['gsl'] + i * 2048, 2048, BF16) for i in (0, 1)]
    cfm = carve('cfm', [128, 1024], BF16)
    cbf = carve('cbf', [128, 8, 16], BF16)
    cbuf = [carve_at(MAP['cb'] + i * 2048, 2048, BF16) for i in (0, 1)]
    aff = carve('aff', [128, 8, 16], F32)
    wab = carve('wab', [128, 16, 16], BF16)
    rgbT_sb = carve('rgbT', [128, 8, 16], F32)
    rb1T_sb = carve('rb1T', [128, 8, 16], F32)
    sgbT_sb = carve('sgbT', [128, 8, 2], F32)
    sb1T_sb = carve('sb1T', [128, 8, 2], F32)
    rb2b = carve('rb2b', [128, 2048], BF16)
    sb2b = carve('sb2b', [128, 2048], BF16)
    sqs = carve('sqs', [128, 512], BF16)
    tks = carve('tks', [128, 64], F32)
    rstd3 = carve_at(MAP['rstd3'], 2048, F32)
    mrs3 = carve_at(MAP['rstd3'] + 2048, 2048, F32)
    acc = carve('acc', [128, 16, 1024], F32)
    tmp3 = carve('tmp3', [128, 16, 512], F32)

    for dst, srcD in ((rgbT_sb, rgbT), (rb1T_sb, rb1T), (sgbT_sb, sgbT), (sb1T_sb, sb1T)):
        SSM.inc(sy.dma_start(out=dst, in_=srcD), 16)
    wa_stage = rows[:, 0:256].rearrange('p (a b) -> p a b', a=16)
    SSM.inc(sy.dma_start(out=wa_stage, in_=wa.rearrange('(c p) e -> p c e', p=128)), 16)
    wge(po, SSM)
    SPO.inc(po.tensor_copy(wab, wa_stage), 1)
    wge(sy, SPO)
    SSM.inc(sy.dma_start(out=rows[0:16, 0:2048], in_=rb2), 16)
    wge(po, SSM)
    SPO.inc(po.tensor_copy(rb2b[0:16, :], rows[0:16, 0:2048]), 1)
    wge(sy, SPO)
    SSM.inc(sy.dma_start(out=rows[0:2, 0:2048], in_=sb2), 16)
    wge(po, SSM)
    SPO.inc(po.tensor_copy(sb2b[0:2, :], rows[0:2, 0:2048]), 1)

    wge(ac, SPO)
    for g in range(2):
        gs = slice(g * 512, (g + 1) * 512)
        wge(pe, SAC)
        for cch in range(16):
            wge(ac, SPE)
            SAC.inc(ac.activation(sqs, x1[:, cch, gs], AF.Square), 1)
            wge(pe, SAC)
            ins0 = pe.matmul(PS[0:1, 0, :], ones_bf[:, 0:1], x1[:, cch, gs],
                             start=(cch == 0), stop=(cch == 15))
            ins1 = pe.matmul(PS[0:1, 1, :], ones_bf[:, 0:1], sqs,
                             start=(cch == 0), stop=(cch == 15))
            SPE.inc(ins1, 1)
        mean = rows[0:1, 0:512]
        msq = rows[0:1, 512:1024]
        mean2 = rows[0:1, 1024:1536]
        rstd = rows[0:1, 1536:2048]
        wge(ac, SPE)
        SAC.inc(ac.activation(mean, PS[0:1, 0, :], AF.Copy, scale=1.0 / D), 1)
        SAC.inc(ac.activation(msq, PS[0:1, 1, :], AF.Copy, scale=1.0 / D), 1)
        ac_edge()
        SAC.inc(ac.activation(mean2, mean, AF.Square), 1)
        wge(ve, SAC)
        SDV.inc(ve.scalar_tensor_tensor(msq, msq, 1.0e-5, mean2,
                                        ALU.add, ALU.subtract), 1)
        wge(ac, SDV)
        SAC.inc(ac.activation(mean2, msq, AF.Ln), 1)
        ac_edge()
        SAC.inc(ac.activation(rstd, mean2, AF.Exp, scale=-0.5), 1)
        ac_edge()
        wge(ve, SAC)
        SDV.inc(ve.tensor_mul(mean2, mean, rstd), 1)
        wge(pe, SAC)
        wge(pe, SDV)
        SPE.inc(pe.matmul(PS[:, 2, :], ones_f[0:1, :], rstd, start=True, stop=True), 1)
        SPE.inc(pe.matmul(PS[:, 3, :], ones_f[0:1, :], mean2, start=True, stop=True), 1)
        wge(ac, SPE)
        SAC.inc(ac.activation(rstd3[:, 0:512], PS[:, 2, :], AF.Copy), 1)
        SAC.inc(ac.activation(mrs3[:, 0:512], PS[:, 3, :], AF.Copy), 1)
        wge(po, SAC)
        for cch in range(16):
            ins = po.tensor_mul(tmp3[:, cch, :], x1[:, cch, gs], rstd3[:, 0:512])
        SPO.inc(ins, 1)
        wge(ve, SPO)
        for cch in range(16):
            ins = ve.tensor_sub(ln2[:, cch, gs], tmp3[:, cch, :], mrs3[:, 0:512])
        SDV.inc(ins, 1)
        wge(po, SDV)
        SPO.inc(po.tensor_tensor(acc[:, :, gs], x1[:, :, gs], ln2[:, :, gs], ALU.add), 1)

    wge(pe, SDV)
    wge(pe, SPO)
    for tt in range(8):
        b = 4 + tt % 2
        wge(pe, SAC)
        wge(pe, SDV)
        for ic in range(16):
            ins = pe.matmul(PS[0:128, b, 0:16], ln2[:, ic, tt * 128:(tt + 1) * 128],
                            wab[:, ic, :], start=(ic == 0), stop=(ic == 15))
        SPE.inc(ins, 1)
        wge(ac, SPE)
        SAC.inc(ac.activation(aff[:, tt, :], PS[0:128, b, 0:16], AF.Sigmoid), 1)
        a_t = aff[:, tt, :]
        t16a, t16b = tks[:, 16:32], tks[:, 32:48]
        m1, m2, den, rp, p1, p2 = (tks[:, 0:1], tks[:, 1:2], tks[:, 2:3],
                                   tks[:, 3:4], tks[:, 4:5], tks[:, 5:6])
        wge(ve, SAC)
        SDV.inc(ve.reduce_max(m1, a_t, axis=AX.X), 1)
        dve_edge()
        SDV.inc(ve.tensor_scalar(t16a, a_t, m1, None, ALU.is_ge), 1)
        dve_edge()
        SDV.inc(ve.scalar_tensor_tensor(t16b, t16a, NEG, a_t, ALU.mult, ALU.add), 1)
        dve_edge()
        SDV.inc(ve.reduce_max(m2, t16b, axis=AX.X), 1)
        dve_edge()
        SDV.inc(ve.tensor_scalar(t16b, a_t, m2, None, ALU.is_ge), 1)
        dve_edge()
        SDV.inc(ve.tensor_sub(t16b, t16b, t16a), 1)
        SDV.inc(ve.tensor_add(den, m1, m2), 1)
        dve_edge()
        SDV.inc(ve.reciprocal(rp, den), 1)
        dve_edge()
        SDV.inc(ve.tensor_mul(p1, m1, rp), 1)
        SDV.inc(ve.tensor_mul(p2, m2, rp), 1)
        dve_edge()
        SDV.inc(ve.tensor_scalar(t16a, t16a, p1, None, ALU.mult), 1)
        SDV.inc(ve.tensor_scalar(t16b, t16b, p2, None, ALU.mult), 1)
        dve_edge()
        SDV.inc(ve.tensor_add(a_t, t16a, t16b), 1)
        dve_edge()
        SDV.inc(ve.tensor_copy(cbf[:, tt, :], a_t), 1)
    wge(pe, SDV)
    wge(pe, SAC)
    for tt in range(8):
        ins = pe.transpose(bankbf(6)[0:16, tt * 128:(tt + 1) * 128], cbf[:, tt, :], ident)
    SPE.inc(ins, 1)
    wge(ac, SPE)
    SAC.inc(ac.activation(cfm[0:16, :], bankbf(6)[0:16, :], AF.Copy), 1)

    wge(pe, SAC)
    for tt in range(8):
        bb = 2 + tt // 4
        ins = pe.matmul(PS[:, bb, (tt % 4) * 128:(tt % 4 + 1) * 128],
                        cbf[:, tt, 0:1].broadcast_to([128, 128]), ident,
                        start=True, stop=True)
    SPE.inc(ins, 1)
    wge(ac, SPE)
    SAC.inc(ac.activation(cbuf[0][:, 0:512], bankf(2), AF.Copy), 1)
    SAC.inc(ac.activation(cbuf[0][:, 512:1024], bankf(3), AF.Copy), 1)
    cb_gate = SAC.n

    def w_src(e, kind):
        if e < NE:
            return {'g': rg, 'w1': rw1, 'w2': rw2}[kind][e]
        return {'g': sg, 'w1': sw1, 'w2': sw2}[kind][e - NE]

    def gbias(e, fc):
        return rgbT_sb[:, fc, e:e + 1] if e < NE else sgbT_sb[:, fc, e - NE:e - NE + 1]

    def bbias(e, fc):
        return rb1T_sb[:, fc, e:e + 1] if e < NE else sb1T_sb[:, fc, e - NE:e - NE + 1]

    stg3 = [carve_at(MAP['rstd3'] + i * 4096, 4096, F32) for i in range(3)]
    stg_state = [None, None, None]
    stg_rr = [0]
    cast_rr = [0]

    def stream_chunk(src, dest_flat, reshape3, gate):
        s = stg_rr[0]
        stg_rr[0] = (s + 1) % 3
        if stg_state[s] is not None:
            wge(sy, stg_state[s][0], stg_state[s][1])
        dview = stg3[s].rearrange('p (c f) -> p c f', c=8) if reshape3 else stg3[s]
        SWs[s].inc(sy.dma_start(out=dview, in_=src), 16)
        if cast_rr[0] == 0:
            wge(ac, SWs[s])
            wge(ac, SPE, gate)
            SAC.inc(ac.activation(dest_flat, stg3[s], AF.Copy), 1)
            stg_state[s] = (SAC, SAC.n)
        else:
            wge(ve, SWs[s])
            wge(ve, SPE, gate)
            SDV.inc(ve.tensor_copy(dest_flat, stg3[s]), 1)
            stg_state[s] = (SDV, SDV.n)
        cast_rr[0] ^= 1

    def emit_wdma(e, fc, kind, dest, gate=0):
        if kind in ('g', 'w1'):
            src_full = w_src(e, kind)[:, fc * 128:(fc + 1) * 128]
            for hh in range(2):
                src = src_full[1024 * hh:1024 * (hh + 1), :].rearrange(
                    '(c p) f -> p c f', p=128)
                stream_chunk(src,
                             dest[:, 8 * hh:8 * (hh + 1), :].rearrange('p a b -> p (a b)'),
                             True, gate)
        else:
            src_full = w_src(e, 'w2')[fc * 128:(fc + 1) * 128, :]
            for hh in range(2):
                stream_chunk(src_full[:, 1024 * hh:1024 * (hh + 1)],
                             dest[:, 1024 * hh:1024 * (hh + 1)], False, gate)

    wge(sy, SPO)
    wge(sy, SDV)
    wge(ac, SPO)
    wge(ac, SDV)
    wge(ve, SPO)
    for fc0 in range(8):
        emit_wdma(0, fc0, 'w2', rw2b[:, fc0, :])
    w2_ready = (SAC.n, SDV.n)
    emit_wdma(0, 0, 'g', rgb_[0])
    emit_wdma(0, 0, 'w1', rw1b[0])
    w_ready = [(SAC.n, SDV.n), (0, 0)]

    b01_gate = SAC.n
    b23_gate_dve = SDV.n
    b23_gate_ac = cb_gate
    out_war = [(SDV, 0), (SDV, 0)]
    gsl_gate = SDV.n
    rgb_free = [0, 0]
    rw1_free = [0, 0]
    rw2_free = 0

    for e in range(NEXP):
        routed = e < NE
        for fc in range(8):
            wpar = fc % 2
            if fc + 1 < 8:
                p2 = (fc + 1) % 2
                emit_wdma(e, fc + 1, 'g', rgb_[p2], gate=rgb_free[p2])
                emit_wdma(e, fc + 1, 'w1', rw1b[p2], gate=rw1_free[p2])
                w_ready[p2] = (SAC.n, SDV.n)
            elif e + 1 < NEXP:
                emit_wdma(e + 1, 0, 'g', rgb_[0], gate=rgb_free[0])
                emit_wdma(e + 1, 0, 'w1', rw1b[0], gate=rw1_free[0])
                w_ready[0] = (SAC.n, SDV.n)
            wge(pe, SAC, max(b01_gate, w_ready[wpar][0]))
            wge(pe, SDV, w_ready[wpar][1])
            for ic in range(16):
                pe.matmul(bankf(0), rgb_[wpar][:, ic, :], ln2[:, ic, 0:512],
                          start=(ic == 0), stop=(ic == 15))
                i1 = pe.matmul(bankf(1), rgb_[wpar][:, ic, :], ln2[:, ic, 512:1024],
                               start=(ic == 0), stop=(ic == 15))
            SPE.inc(i1, 1)
            rgb_free[wpar] = SPE.n
            wge(ac, SPE)
            wge(ac, SDV, gsl_gate)
            for th in range(2):
                SAC.inc(ac.activation(gsl[wpar][:, th * 512:(th + 1) * 512], bankf(th),
                                      AF.Gelu, bias=gbias(e, fc)), 1)
            b01_gate = SAC.n
            wge(pe, SDV, max(b23_gate_dve, w_ready[wpar][1]))
            wge(pe, SAC, b23_gate_ac)
            for ic in range(16):
                pe.matmul(bankf(2), rw1b[wpar][:, ic, :], ln2[:, ic, 0:512],
                          start=(ic == 0), stop=(ic == 15))
                i1 = pe.matmul(bankf(3), rw1b[wpar][:, ic, :], ln2[:, ic, 512:1024],
                               start=(ic == 0), stop=(ic == 15))
            SPE.inc(i1, 1)
            rw1_free[wpar] = SPE.n
            wge(ve, SPE)
            wge(ve, SAC, b01_gate)
            for th in range(2):
                SDV.inc(ve.scalar_tensor_tensor(hsl[:, fc, th * 512:(th + 1) * 512],
                                                bankf(2 + th), bbias(e, fc),
                                                gsl[wpar][:, th * 512:(th + 1) * 512],
                                                ALU.add, ALU.mult), 1)
            b23_gate_dve = SDV.n
            gsl_gate = SDV.n
            if routed:
                wge(po, SDV, b23_gate_dve)
                wge(po, SAC, b23_gate_ac)
                SPO.inc(po.tensor_tensor(hsl[:, fc, :], hsl[:, fc, :], cbuf[e % 2], ALU.mult), 1)
        h_dve, h_po = b23_gate_dve, SPO.n
        if e + 1 < NE:
            en = e + 1
            wge(pe, SDV, b23_gate_dve)
            for tt in range(8):
                bb = 2 + tt // 4
                ins = pe.matmul(PS[:, bb, (tt % 4) * 128:(tt % 4 + 1) * 128],
                                cbf[:, tt, en:en + 1].broadcast_to([128, 128]), ident,
                                start=True, stop=True)
            SPE.inc(ins, 1)
            wge(ac, SPE)
            wge(ac, SPO, h_po)
            SAC.inc(ac.activation(cbuf[en % 2][:, 0:512], bankf(2), AF.Copy), 1)
            SAC.inc(ac.activation(cbuf[en % 2][:, 512:1024], bankf(3), AF.Copy), 1)
            b23_gate_ac = SAC.n
        wge(pe, SDV, h_dve)
        wge(pe, SPO, h_po)
        wge(pe, SAC, w2_ready[0])
        wge(pe, SDV, w2_ready[1])
        for dc in range(16):
            pr = dc % 2
            bb0, bb1 = (4, 5) if pr == 0 else (6, 7)
            ws, wv = out_war[pr]
            wge(pe, ws, wv)
            for fc in range(8):
                pe.matmul(bankf(bb0), rw2b[:, fc, dc * 128:(dc + 1) * 128],
                          hsl[:, fc, 0:512], start=(fc == 0), stop=(fc == 7))
                i1 = pe.matmul(bankf(bb1), rw2b[:, fc, dc * 128:(dc + 1) * 128],
                               hsl[:, fc, 512:1024], start=(fc == 0), stop=(fc == 7))
            SPE.inc(i1, 1)
            wge(ve, SPE)
            SDV.inc(ve.tensor_tensor(acc[:, dc, :], acc[:, dc, :],
                                     PS[:, bb0:bb0 + 2, :].rearrange('p a b -> p (a b)'),
                                     ALU.add), 1)
            out_war[pr] = (SDV, SDV.n)
        rw2_free = SPE.n
        if e + 1 < NEXP:
            for fc0 in range(8):
                emit_wdma(e + 1, fc0, 'w2', rw2b[:, fc0, :], gate=rw2_free)
            w2_ready = (SAC.n, SDV.n)

    for dc in range(16):
        pr = dc % 2
        bb0, bb1 = (4, 5) if pr == 0 else (6, 7)
        ws, wv = out_war[pr]
        wge(pe, ws, wv)
        wge(pe, SAC)
        for th in range(2):
            bb = bb0 if th == 0 else bb1
            pe.matmul(bankf(bb), rb2b[0:16, dc * 128:(dc + 1) * 128],
                      cfm[0:16, th * 512:(th + 1) * 512], start=True, stop=False)
            ins = pe.matmul(bankf(bb), sb2b[0:2, dc * 128:(dc + 1) * 128],
                            ones_bf[0:2, th * 512:(th + 1) * 512], start=False, stop=True)
        SPE.inc(ins, 1)
        wge(ve, SPE)
        SDV.inc(ve.tensor_tensor(acc[:, dc, :], acc[:, dc, :],
                                 PS[:, bb0:bb0 + 2, :].rearrange('p a b -> p (a b)'),
                                 ALU.add), 1)
        out_war[pr] = (SDV, SDV.n)

    wge(sy, SDV)
    SOUT.inc(sy.dma_start(out=o.rearrange('(c p) t -> p c t', p=128), in_=acc), 16)
    wge(sy, SOUT)
    return nc


def kernel(**inputs):
    f32 = np.float32
    bf = ml_dtypes.bfloat16
    x = np.asarray(inputs['L0_attn_x'], f32)
    Wq = np.asarray(inputs['L0_attn_Wq_w'], f32)
    Wdkv = np.asarray(inputs['L0_attn_Wdkv_w'], f32)
    Wuk = np.asarray(inputs['L0_attn_Wuk_w'], f32)
    Wuv = np.asarray(inputs['L0_attn_Wuv_w'], f32)
    Wo = np.asarray(inputs['L0_attn_Wo_w'], f32)

    def biasT(v, nch):
        E = v.shape[0]
        return np.ascontiguousarray(v.T.reshape(nch, 128, E).transpose(1, 0, 2))

    shared = dict(
        wq=np.ascontiguousarray(Wq),
        wqt=np.ascontiguousarray(Wq.T),
        wuk=np.ascontiguousarray(Wuk),
        wuv=np.ascontiguousarray(Wuv),
        wot=np.ascontiguousarray(Wo.T),
        wdkvt=np.ascontiguousarray(Wdkv.T),
        rg=np.ascontiguousarray(np.asarray(inputs['L0_ffn_rg'], f32)),
        rw1=np.ascontiguousarray(np.asarray(inputs['L0_ffn_rw1'], f32)),
        rw2=np.ascontiguousarray(np.asarray(inputs['L0_ffn_rw2'], f32)),
        sg=np.ascontiguousarray(np.asarray(inputs['L0_ffn_sg'], f32)),
        sw1=np.ascontiguousarray(np.asarray(inputs['L0_ffn_sw1'], f32)),
        sw2=np.ascontiguousarray(np.asarray(inputs['L0_ffn_sw2'], f32)),
        wa=np.ascontiguousarray(np.asarray(inputs['L0_ffn_wa'], f32)),
        rgbT=biasT(np.asarray(inputs['L0_ffn_rgb'], f32), 8),
        rb1T=biasT(np.asarray(inputs['L0_ffn_rb1'], f32), 8),
        sgbT=biasT(np.asarray(inputs['L0_ffn_sgb'], f32), 8),
        sb1T=biasT(np.asarray(inputs['L0_ffn_sb1'], f32), 8),
        rb2=np.ascontiguousarray(np.asarray(inputs['L0_ffn_rb2'], f32)),
        sb2=np.ascontiguousarray(np.asarray(inputs['L0_ffn_sb2'], f32)),
        identD=np.eye(128, dtype=f32).astype(bf),
    )

    xT = [np.ascontiguousarray(x[b].T) for b in range(B)]

    in_maps, tiles = [], []
    for k in range(8):
        b, r = k // 4, k % 4
        Ts = [4 * (c - 1) + ((r + c - 1) % 4) for c in range(1, NSLOT + 1)]
        tiles.append((b, Ts))
        xo = np.ascontiguousarray(
            np.concatenate([xT[b][:, 128 * t:128 * t + 128] for t in Ts], axis=1))
        kk = np.arange(512)[None, :]
        qq = np.arange(128)[:, None]
        mk = np.zeros((128, NSLOT, 512), dtype=f32)
        for s in range(NSLOT):
            m = (r + s) % 4
            mk[:, s, :] = np.where(kk <= 128 * m + qq, 0.0, NEG)
        im = dict(shared)
        im['xall'] = xT[b]
        im['xown'] = xo
        im['masksD'] = mk.astype(bf)
        in_maps.append(im)

    nc = build_program()
    res = run_bass_kernel_spmd(nc, in_maps, core_ids=list(range(8)))

    out = np.empty((B, T, D), dtype=f32)
    for k in range(8):
        b, Ts = tiles[k]
        ok = res.results[k]['o']
        for s, t in enumerate(Ts):
            out[b, 128 * t:128 * t + 128, :] = ok[:, 128 * s:128 * (s + 1)].T
    return out
